# version 06
# baseline (speedup 1.0000x reference)
import numpy as np
import ml_dtypes
from contextlib import ExitStack

import concourse.bass as bass
import concourse.mybir as mybir
from concourse.bass_utils import run_bass_kernel_spmd

F32 = mybir.dt.float32
BF16 = mybir.dt.bfloat16
I8 = mybir.dt.int8
U8 = mybir.dt.uint8
AF = mybir.ActivationFunctionType
OP = mybir.AluOpType
AXX = mybir.AxisListType.X

DEFER_PV = False
DEFER_FROM_HEAD = 0
GATHER_HEADS_CFG = 12
AG_MIN_HEAD = 0
H = 2048
LL = 512
RD = 64
NH = 16
DH = 128
S = 8192
QT = 1024
INTER = 5504
NCORE = 8
NKT = 64
SCALE = 1.0 / float(np.sqrt(DH + RD))
EXP_BIAS = -20.0
EPS = 1e-5
NTILE = 688
SH_T = NTILE // NCORE
GU_T = 43
GU_F = GU_T * 128


class P:
    def __init__(self, nc):
        self.nc = nc
        self.sems = {}
        self.cnt = {}
        self.E = None
        self.eng = None
        self.last_inst = None
        self.last_inst_inc = None

    def mksem(self, ctx, names):
        for n in names:
            self.sems[n] = ctx.enter_context(self.nc.semaphore(n))
            self.cnt[n] = 0

    def reset(self):
        self.last_inst = None
        self.last_inst_inc = None
        for k in self.cnt:
            self.cnt[k] = 0

    def on(self, name):
        return self.E == name

    def inc(self, sem, n, inst):
        self.cnt[sem] += n
        if inst is not None:
            inst.then_inc(self.sems[sem], n)
            if inst is self.last_inst:
                self.last_inst_inc = (sem, self.cnt[sem])
        return self.cnt[sem]

    def wait(self, eng, sem, val):
        if val <= 0:
            return
        if self.on(eng):
            self.eng.wait_ge(self.sems[sem], val)

    def _completion_anchor(self):
        if self.last_inst is None:
            return self.eng.nop()
        if self.last_inst_inc is None:
            return self.last_inst
        sem, val = self.last_inst_inc
        self.eng.wait_ge(self.sems[sem], val)
        ret = self.eng.nop()
        self.last_inst = None
        return ret

    def mark(self, eng, sem):
        inst = self._completion_anchor() if self.on(eng) else None
        return self.inc(sem, 1, inst)

    def dma(self, queue, out, in_, sem="ld", **kw):
        inst = None
        if self.on(queue):
            inst = self.eng.dma_start(out=out, in_=in_, **kw)
            self.last_inst = inst
            self.last_inst_inc = None
        return self.inc(sem, 16, inst)

    def dmaT(self, queue, out, in_, sem="tp"):
        inst = None
        if self.on(queue):
            inst = self.eng.dma_start(out=out, in_=in_, transpose=True)
        return self.inc(sem, 16, inst)

    def mm(self, out, lhsT, rhs, start, stop, sem=None):
        inst = None
        if self.on("tensor"):
            inst = self.nc.tensor.matmul(out, lhsT, rhs, start=start, stop=stop)
            self.last_inst = inst
            self.last_inst_inc = None
        if sem is not None:
            return self.inc(sem, 1, inst)
        return None

    def tr(self, out, in_, ident, sem=None):
        inst = None
        if self.on("tensor"):
            inst = self.nc.tensor.transpose(out, in_, ident)
            self.last_inst = inst
            self.last_inst_inc = None
        if sem is not None:
            return self.inc(sem, 1, inst)
        return None

    def act(self, out, in_, func, sem=None, **kw):
        inst = None
        if self.on("scalar"):
            inst = self.nc.scalar.activation(out, in_, func, **kw)
            self.last_inst = inst
            self.last_inst_inc = None
        if sem is not None:
            return self.inc(sem, 1, inst)
        return None

    def vec(self, meth, *args, sem=None, **kw):
        inst = None
        if self.on("vector"):
            inst = getattr(self.nc.vector, meth)(*args, **kw)
            self.last_inst = inst
            self.last_inst_inc = None
        if sem is not None:
            return self.inc(sem, 1, inst)
        return None

    def barrier(self):
        base = self.cnt["bar"]
        if self.eng is not None:
            if self.E in ("tensor", "vector", "scalar"):
                inst = self._completion_anchor()
            else:
                inst = self.eng.nop()
            inst.then_inc(self.sems["bar"], 1)
        self.cnt["bar"] = base + 5
        if self.eng is not None:
            self.eng.wait_ge(self.sems["bar"], base + 5)


def build_program(debug=False, stop_after=None, nh_lim=None):
    nc = bass.Bass("TRN2", target_bir_lowering=False, num_devices=NCORE)

    d_hkv = nc.dram_tensor("hkv", [QT, H], F32, kind="ExternalInput").ap()
    d_hq = nc.dram_tensor("hq", [QT, H], F32, kind="ExternalInput").ap()
    d_coskv = nc.dram_tensor("coskv", [64, QT], BF16, kind="ExternalInput").ap()
    d_coskv2 = nc.dram_tensor("coskv2", [64, QT], BF16, kind="ExternalInput").ap()
    d_wdkv = nc.dram_tensor("wdkv", [H, LL], F32, kind="ExternalInput").ap()
    d_wuk = nc.dram_tensor("wuk", [LL, H], F32, kind="ExternalInput").ap()
    d_wkr = nc.dram_tensor("wkr", [H, RD], F32, kind="ExternalInput").ap()
    d_wuv = nc.dram_tensor("wuv", [LL, H], F32, kind="ExternalInput").ap()
    d_wdq = nc.dram_tensor("wdq", [H, LL], F32, kind="ExternalInput").ap()
    d_wuq = nc.dram_tensor("wuq", [LL, H], F32, kind="ExternalInput").ap()
    d_wqr = nc.dram_tensor("wqr", [LL, NH * RD], F32, kind="ExternalInput").ap()
    d_wo = nc.dram_tensor("wo", [H, H], F32, kind="ExternalInput").ap()
    d_lut = nc.dram_tensor("lut16", [1, 16], F32, kind="ExternalInput").ap()
    d_gsh = nc.dram_tensor("gsh", [128, SH_T * 128], I8, kind="ExternalInput").ap()
    d_ush = nc.dram_tensor("ush", [128, SH_T * 128], I8, kind="ExternalInput").ap()
    d_dsh = nc.dram_tensor("dsh", [128, SH_T * 128], I8, kind="ExternalInput").ap()
    d_masks = nc.dram_tensor("masks", [128, NKT * 128], U8, kind="ExternalInput").ap()
    d_cosq = nc.dram_tensor("cosq", [64, QT], F32, kind="ExternalInput").ap()
    d_cosq2 = nc.dram_tensor("cosq2", [64, QT], F32, kind="ExternalInput").ap()
    d_out = nc.dram_tensor("out", [QT, H], F32, kind="ExternalOutput").ap()
    taps = {}
    if debug:
        for nm, sh, dt in [("t_ckv", [128, 4 * QT], BF16), ("t_kr", [64, S], BF16),
                           ("t_qc", [128, NH * QT], BF16), ("t_qr", [64, NH * QT], BF16),
                           ("t_oall", [128, NH * QT], BF16),
                           ("t_z1", [128, 16 * QT], F32), ("t_z2", [128, 16 * QT], F32),
                           ("t_zln", [128, 16 * QT], BF16), ("t_act", [128, 43 * 512], BF16),
                           ("t_z3", [128, 16 * QT], F32),
                           ("t_oall2", [128, NH * QT], BF16),
                           ("t_ccg", [8, 128, 128], BF16),
                           ("t_ccd", [8, 128, 128], BF16),
                           ("t_ccge", [16, 128, 128], BF16),
                           ("t_wgt", [128, 4 * 16 * 128], BF16),
                           ("t_wut", [128, 4 * 16 * 128], BF16),
                           ("t_psg", [128, 512], F32),
                           ("t_psu", [128, 512], F32)]:
            taps[nm] = nc.dram_tensor(nm, sh, dt, kind="ExternalOutput").ap()

    cc_in = [nc.dram_tensor(f"cc_in_{m}", [SH_T, 128, 128], BF16) for m in "gud"]
    cc_out = [nc.dram_tensor(f"cc_out_{m}", [NTILE, 128, 128], BF16) for m in "gud"]
    cc_kvin = nc.dram_tensor("cc_kvin", [64, QT], BF16)
    cc_kvout = nc.dram_tensor("cc_kvout", [8, 64, QT], BF16)
    cc_xin = [nc.dram_tensor(f"cc_xin{i}", [2, 128, QT], BF16) for i in range(2)]
    cc_xout = [nc.dram_tensor(f"cc_xout{i}", [16, 128, QT], BF16) for i in range(2)]

    m1 = np.zeros((64, 64), dtype=np.float64)
    m2 = np.zeros((64, 64), dtype=np.float64)
    for j in range(32):
        m1[j, j] = 1.0
        m1[32 + j, j] = -1.0
        m2[j, 32 + j] = 1.0
        m2[32 + j, 32 + j] = 1.0
    i_m1 = nc.inline_tensor(m1.astype(ml_dtypes.bfloat16), "ropeM1")
    i_m2 = nc.inline_tensor(m2.astype(ml_dtypes.bfloat16), "ropeM2")
    i_identf = nc.inline_tensor(np.eye(128, dtype=np.float32), "identf")
    i_ones_bf = nc.inline_tensor(np.ones((128, 1), dtype=ml_dtypes.bfloat16), "ones_bf")
    i_ones_f = nc.inline_tensor(np.ones((128, 1), dtype=np.float32), "ones_f")
    i_onesrow_f = nc.inline_tensor(np.ones((1, 128), dtype=np.float32), "onesrow_f")

    p = P(nc)

    with ExitStack() as ctx:
        p.mksem(ctx, [
            "ld", "gld", "tp", "sLN", "sSq", "sPA", "sEvA", "sLut", "sGat",
            "sAG", "sKV", "sKX", "sExp", "sEvc", "sS", "sP", "sM", "sF", "sHd",
            "sZ", "sL2", "sFF", "sOut", "bar",
        ])

        A = lambda nm, sh, dt: ctx.enter_context(nc.sbuf_tensor(nm, sh, dt)).ap()
        lutsb = A("lutsb", [1, 16], F32)
        lutb = A("lutb", [128, 16], F32)
        onesb = A("onesb", [128, 1], BF16)
        onesf = A("onesf", [128, 1], F32)
        onesrf = A("onesrf", [1, 128], F32)
        identf = A("identf_s", [128, 128], F32)
        ebias = A("ebias", [128, 1], F32)
        mrg1 = A("mrg1", [64, 64], BF16)
        mrg2 = A("mrg2", [64, 64], BF16)
        arL1 = nc.auto_arena()
        cKV = arL1.sbuf_tensor("cKV", [128, 4, QT], BF16).ap()
        kRb = arL1.sbuf_tensor("kRb", [64, S], BF16).ap()
        arO = nc.auto_arena()
        o_all = arO.sbuf_tensor("o_all", [128, NH * QT], BF16).ap()

        arA = nc.auto_arena()
        hbuf = [arA.sbuf_tensor(f"hbuf{i}", [128, H], F32).ap() for i in range(2)]
        hln = [arA.sbuf_tensor(f"hln{i}", [128, H], BF16).ap() for i in range(2)]
        hlnT = arA.sbuf_tensor("hlnT", [128, 16, 256], BF16).ap()
        stat = arA.sbuf_tensor("stat", [128, 12], F32).ap()
        ropeT = arA.sbuf_tensor("ropeT", [64, 512], BF16).ap()
        arA1 = nc.auto_arena()
        wdkvb = arA1.sbuf_tensor("wdkvb", [128, 16, LL], BF16).ap()
        wkrb = arA1.sbuf_tensor("wkrb", [128, 16, RD], BF16).ap()
        cosks = arA1.sbuf_tensor("cosks", [64, QT], BF16).ap()
        cosks2 = arA1.sbuf_tensor("cosks2", [64, QT], BF16).ap()
        wdqb = arA1.sbuf_tensor("wdqb", [128, 16, LL], BF16).ap()
        arL2 = nc.auto_arena()
        QC = arL2.sbuf_tensor("QC", [128, NH * QT], BF16).ap()
        QR = arL2.sbuf_tensor("QR", [64, NH * QT], BF16).ap()
        KCb = [arL2.sbuf_tensor(f"KCb{i}", [128, 1024], BF16).ap() for i in range(2)]
        Vb = [arL2.sbuf_tensor(f"Vb{i}", [128, 1024], BF16).ap() for i in range(2)]
        kstage = arL2.sbuf_tensor("kstage", [128, 2048], BF16).ap()
        arA2b = nc.auto_arena()
        wuqb = arA2b.sbuf_tensor("wuqb", [128, 4, H], BF16).ap()
        wqrb = arA2b.sbuf_tensor("wqrb", [128, 4, NH * RD], BF16).ap()
        csq = arA2b.sbuf_tensor("csq", [64, QT], F32).ap()
        csq2 = arA2b.sbuf_tensor("csq2", [64, QT], F32).ap()
        ropeTq = arA2b.sbuf_tensor("ropeTq", [64, 2 * QT], BF16).ap()
        arRcq = nc.auto_arena()
        cQ = arRcq.sbuf_tensor("cQ", [128, 4, QT], BF16).ap()

        arB = nc.auto_arena()
        maskb = arB.sbuf_tensor("maskb", [128, NKT * 128], U8).ap()
        ztile = arB.sbuf_tensor("ztile", [128, 128], BF16).ap()
        wukb = [arB.sbuf_tensor(f"wukb{i}", [128, 4, DH], BF16).ap() for i in range(3)]
        wuvb = [arB.sbuf_tensor(f"wuvb{i}", [128, 4, DH], BF16).ap() for i in range(3)]
        probs = [arB.sbuf_tensor(f"probs{i}", [128, QT], BF16).ap() for i in range(3)]
        rn = arB.sbuf_tensor("rn", [1, QT], F32).ap()
        rnb = arB.sbuf_tensor("rnb", [128, 256], F32).ap()
        gidxb = [arB.sbuf_tensor(f"gidxb{i}", [128, 256], BF16).ap() for i in range(2)]
        gacc = [arB.sbuf_tensor(f"gacc{i}", [128, 256], BF16).ap() for i in range(2)]
        gtmp = [arB.sbuf_tensor(f"gtmp{i}", [128, 256], BF16).ap()
                for i in range(2)]

        arC = nc.auto_arena()
        z = arC.sbuf_tensor("z", [128, 16, QT], F32).ap()
        hbufC = arC.sbuf_tensor("hbufC", [128, H], F32).ap()
        arC1 = nc.auto_arena()
        wotile = [arC1.sbuf_tensor(f"wot{i}", [128, 16, 128], BF16).ap() for i in range(2)]
        arC2 = nc.auto_arena()
        zln = arC2.sbuf_tensor("zln", [128, 16, QT], BF16).ap()
        actb = arC2.sbuf_tensor("actb", [128, 43, 512], BF16).ap()
        sqz = arC2.sbuf_tensor("sqz", [128, QT], F32).ap()
        gsl = arC2.sbuf_tensor("gsl", [128, 512], BF16).ap()
        l2s = arC2.sbuf_tensor("l2s", [1, 2 * QT], F32).ap()
        isdt = arC2.sbuf_tensor("isdt", [1, QT], F32).ap()
        wgt = [arC2.sbuf_tensor(f"wgt{i}", [128, 16, 128], BF16).ap() for i in range(2)]
        wut = [arC2.sbuf_tensor(f"wut{i}", [128, 16, 128], BF16).ap() for i in range(2)]
        wdt = [arC2.sbuf_tensor(f"wdt{i}", [128, 43, 128], BF16).ap() for i in range(2)]
        arC3 = nc.auto_arena()
        outb = [arC3.sbuf_tensor(f"outb{i}", [128, H], F32).ap() for i in range(2)]

        arL1.allocate("left")
        arRcq.allocate("right")
        arA.allocate("left")
        arA1.allocate("left"); arA1.free()
        arA.free()
        arL2.allocate("left")
        arA2b.allocate("left"); arA2b.free()
        arRcq.free()
        arO.allocate("right")
        arB.allocate("right"); arB.free()
        arL2.free()
        arL1.free()
        arC.allocate("left")
        arC1.allocate("right"); arC1.free()
        arO.free()
        arC2.allocate("left"); arC2.free()
        arC3.allocate("left"); arC3.free()
        arC.free()

        stP1 = ExitStack()
        psA = [stP1.enter_context(nc.psum_tensor(f"psA{i}", [128, 256], F32)).ap()
               for i in range(4)]
        psB = stP1.enter_context(nc.psum_tensor("psB", [64, 256], F32)).ap()
        psB2 = stP1.enter_context(nc.psum_tensor("psB2", [64, 256], F32)).ap()
        psLut = stP1.enter_context(nc.psum_tensor("psLut", [128, 16], F32)).ap()
        stP1.close()
        stP2 = ExitStack()
        psC = stP2.enter_context(nc.psum_tensor("psC", [128, 1024], F32)).ap()
        psR = stP2.enter_context(nc.psum_tensor("psR", [64, 1024], F32)).ap()
        psR2 = stP2.enter_context(nc.psum_tensor("psR2", [64, 1024], F32)).ap()
        stP2.close()
        stP3 = ExitStack()
        psSa = stP3.enter_context(nc.psum_tensor("psSa", [128, 1024], F32)).ap()
        psSb = stP3.enter_context(nc.psum_tensor("psSb", [128, 1024], F32)).ap()
        psO = stP3.enter_context(nc.psum_tensor("psO", [128, 1024], F32)).ap()
        psSum = stP3.enter_context(nc.psum_tensor("psSum", [1, 1024], F32)).ap()
        stP3.close()
        stP4 = ExitStack()
        psW = [stP4.enter_context(nc.psum_tensor(f"psW{i}", [128, 1024], F32)).ap()
               for i in range(2)]
        ps3 = stP4.enter_context(nc.psum_tensor("ps3", [128, 512], F32)).ap()
        psT = [stP4.enter_context(nc.psum_tensor(f"psT{i}", [128, 128], F32)).ap()
               for i in range(2)]
        stP4.close()

        def walk():
            p.dma("sync", identf, i_identf.ap())
            p.dma("sync", onesrf, i_onesrow_f.ap())
            p.dma("sync", cosks, d_coskv)
            p.dma("sync", cosks2, d_coskv2)
            p.dma("sync", mrg1, i_m1.ap())
            ld_pre = p.dma("sync", mrg2, i_m2.ap())
            p.dma("gpsimd", onesb, i_ones_bf.ap(), sem="gld")
            p.dma("gpsimd", onesf, i_ones_f.ap(), sem="gld")
            p.dma("gpsimd", lutsb, d_lut, sem="gld")
            p.dma("gpsimd", wdkvb, d_wdkv.rearrange("(dc q) l -> q dc l", q=128), sem="gld")
            p.dma("gpsimd", wkrb, d_wkr.rearrange("(dc q) r -> q dc r", q=128), sem="gld")
            gld_w = p.dma("gpsimd", wdqb, d_wdq.rearrange("(dc q) l -> q dc l", q=128), sem="gld")

            p.wait("vector", "gld", gld_w)
            p.vec("memset", ebias[:], EXP_BIAS, sem="sLut")
            p.wait("tensor", "gld", gld_w)
            p.wait("tensor", "ld", ld_pre)
            p.mm(psLut[:], onesrf[:], lutsb[:], True, True, sem="sLut")
            p.wait("vector", "sLut", p.cnt["sLut"])
            p.vec("tensor_copy", lutb[:], psLut[:], sem="sLut")
            lut_ready = p.cnt["sLut"]

            last_hbuf = [0, 0]
            last_hbuf_sq = [0, 0]
            last_hln = [0, 0]
            last_hlnT_pa = 0

            def ln_tile(dram_rows, par):
                c0 = par * 6
                p.wait("sync", "sLN", last_hbuf[par])
                p.wait("sync", "sSq", last_hbuf_sq[par])
                lv = p.dma("sync", hbuf[par][:], dram_rows)
                p.wait("vector", "ld", lv)
                p.vec("reduce_sum", stat[:, c0:c0 + 1], hbuf[par][:], axis=AXX)
                muv = p.mark("vector", "sLN")
                p.wait("scalar", "ld", lv)
                p.wait("scalar", "tp", last_hln[par])
                p.act(hln[par][:], hbuf[par][:], AF.Square,
                      accum_out=stat[:, c0 + 1:c0 + 2], sem="sSq")
                sqv = p.cnt["sSq"]
                last_hbuf_sq[par] = sqv
                p.wait("vector", "sLN", muv)
                p.wait("vector", "sSq", sqv)
                p.vec("tensor_scalar", stat[:, c0:c0 + 1], stat[:, c0:c0 + 1],
                      1.0 / H, None, op0=OP.mult)
                mv = p.mark("vector", "sLN")
                p.vec("tensor_scalar", stat[:, c0 + 1:c0 + 2], stat[:, c0 + 1:c0 + 2],
                      1.0 / H, EPS, op0=OP.mult, op1=OP.add)
                s1v = p.mark("vector", "sLN")
                p.wait("scalar", "sLN", mv)
                p.act(stat[:, c0 + 2:c0 + 3], stat[:, c0:c0 + 1], AF.Square, sem="sSq")
                p.wait("vector", "sSq", p.cnt["sSq"])
                p.wait("vector", "sLN", s1v)
                p.vec("tensor_sub", stat[:, c0 + 4:c0 + 5], stat[:, c0 + 1:c0 + 2],
                      stat[:, c0 + 2:c0 + 3])
                vv = p.mark("vector", "sLN")
                p.wait("scalar", "sLN", vv)
                p.act(stat[:, c0 + 4:c0 + 5], stat[:, c0 + 4:c0 + 5], AF.Sqrt, sem="sSq")
                p.wait("vector", "sSq", p.cnt["sSq"])
                p.vec("reciprocal", stat[:, c0 + 3:c0 + 4], stat[:, c0 + 4:c0 + 5])
                rv = p.mark("vector", "sLN")
                p.wait("vector", "sLN", rv)
                p.wait("vector", "tp", last_hln[par])
                p.vec("tensor_scalar", hln[par][:], hbuf[par][:],
                      stat[:, c0:c0 + 1], stat[:, c0 + 3:c0 + 4],
                      op0=OP.subtract, op1=OP.mult)
                lnv = p.mark("vector", "sLN")
                last_hbuf[par] = lnv
                return lnv

            def tr_tile(par, sub, lnv):
                p.wait("scalar", "sLN", lnv)
                p.wait("scalar", "sPA", last_hlnT_pa)
                inst = None
                if p.on("scalar"):
                    inst = p.eng.dma_start(
                        out=hlnT[:, :, sub * 128:(sub + 1) * 128],
                        in_=hln[par].rearrange("t (c d) -> t c d", c=16),
                        transpose=True)
                tv = p.inc("tp", 16, inst)
                last_hln[par] = tv
                return tv

            for g in range(4):
                tpv = 0
                for s_ in range(2):
                    tile = g * 2 + s_
                    lnv = ln_tile(d_hkv[tile * 128:(tile + 1) * 128, :], s_)
                    tpv = tr_tile(s_, s_, lnv)
                p.wait("tensor", "tp", tpv)
                p.wait("tensor", "sEvA", p.cnt["sEvA"])
                for lc in range(4):
                    for dc in range(16):
                        p.mm(psA[lc][:, 0:256],
                             wdkvb[:, dc, lc * 128:(lc + 1) * 128],
                             hlnT[:, dc, 0:256], dc == 0, dc == 15)
                for dc in range(16):
                    p.mm(psB[:, 0:256], wkrb[:, dc, :], hlnT[:, dc, 0:256],
                         dc == 0, dc == 15, sem="sPA" if dc == 15 else None)
                last_hlnT_pa = p.cnt["sPA"]
                pa_g = p.cnt["sPA"]
                p.wait("vector", "sPA", pa_g)
                t0 = g * 256
                for lc in range(4):
                    p.vec("tensor_copy", cKV[:, lc, t0:t0 + 256], psA[lc][:, 0:256])
                p.vec("tensor_mul", ropeT[0:32, 0:256], psB[0:32, 0:256],
                      cosks[0:32, t0:t0 + 256])
                p.vec("tensor_mul", ropeT[32:64, 0:256], psB[32:64, 0:256],
                      cosks[32:64, t0:t0 + 256])
                p.vec("tensor_mul", ropeT[0:32, 256:512], psB[0:32, 0:256],
                      cosks2[0:32, t0:t0 + 256])
                p.vec("tensor_mul", ropeT[32:64, 256:512], psB[32:64, 0:256],
                      cosks2[32:64, t0:t0 + 256], sem="sEvA")
                p.wait("tensor", "sEvA", p.cnt["sEvA"])
                p.mm(psB2[:, 0:256], mrg1[:], ropeT[:, 0:256], True, False)
                p.mm(psB2[:, 0:256], mrg2[:], ropeT[:, 256:512], False, True,
                     sem="sPA")
                p.wait("vector", "sPA", p.cnt["sPA"])
                p.vec("tensor_copy", kRb[:, t0:t0 + 256], psB2[:, 0:256], sem="sEvA")

                tpv = 0
                for s_ in range(2):
                    tile = g * 2 + s_
                    lnv = ln_tile(d_hq[tile * 128:(tile + 1) * 128, :], s_)
                    tpv = tr_tile(s_, s_, lnv)
                p.wait("tensor", "tp", tpv)
                p.wait("tensor", "sEvA", p.cnt["sEvA"])
                for lc in range(4):
                    for dc in range(16):
                        p.mm(psA[lc][:, 0:256],
                             wdqb[:, dc, lc * 128:(lc + 1) * 128],
                             hlnT[:, dc, 0:256], dc == 0, dc == 15,
                             sem="sPA" if (dc == 15 and lc == 3) else None)
                last_hlnT_pa = p.cnt["sPA"]
                p.wait("vector", "sPA", p.cnt["sPA"])
                for lc in range(4):
                    p.vec("tensor_copy", cQ[:, lc, t0:t0 + 256], psA[lc][:, 0:256],
                          sem="sEvA" if lc == 3 else None)

            keys_pa = p.cnt["sPA"]
            keys_ev = p.cnt["sEvA"]
            p.wait("gpsimd", "sPA", keys_pa)
            p.wait("gpsimd", "sEvA", keys_ev)
            kv_st = p.dma("gpsimd", cc_kvin.ap(), kRb[:, 0:QT], sem="gld")
            p.wait("gpsimd", "gld", kv_st)
            inst = None
            if p.on("gpsimd"):
                inst = p.eng.collective_compute(
                    "AllGather", OP.bypass,
                    replica_groups=[list(range(NCORE))],
                    ins=[cc_kvin.ap().opt()],
                    outs=[cc_kvout.ap().opt()])
            p.inc("sKV", 1, inst)

            hq_pa = p.cnt["sPA"]
            hq_ev = p.cnt["sEvA"]
            p.wait("gpsimd", "sPA", hq_pa)
            p.wait("gpsimd", "sEvA", hq_ev)
            p.dma("gpsimd", wuqb, d_wuq.rearrange("(lc q) d -> q lc d", q=128), sem="gld")
            gld_q2 = p.dma("gpsimd", wqrb, d_wqr.rearrange("(lc q) d -> q lc d", q=128), sem="gld")
            p.dma("sync", csq, d_cosq)
            csq_v = p.dma("sync", csq2, d_cosq2)
            p.wait("tensor", "gld", gld_q2)
            p.wait("vector", "ld", csq_v)

            p.wait("tensor", "sEvA", p.cnt["sEvA"])
            for h_ in range(NH):
                if h_ > 0:
                    p.wait("tensor", "sEvA", p.cnt["sEvA"])
                for half in range(2):
                    for lc in range(4):
                        p.mm(psC[:, half * 512:(half + 1) * 512],
                             wuqb[:, lc, h_ * 128:(h_ + 1) * 128],
                             cQ[:, lc, half * 512:(half + 1) * 512],
                             lc == 0, lc == 3)
                    for lc in range(4):
                        p.mm(psR[:, half * 512:(half + 1) * 512],
                             wqrb[:, lc, h_ * RD:(h_ + 1) * RD],
                             cQ[:, lc, half * 512:(half + 1) * 512],
                             lc == 0, lc == 3,
                             sem="sPA" if (lc == 3 and half == 1) else None)
                p.wait("vector", "sPA", p.cnt["sPA"])
                p.vec("tensor_scalar", QC[:, h_ * QT:(h_ + 1) * QT], psC[:],
                      SCALE, None, op0=OP.mult)
                p.vec("tensor_mul", ropeTq[0:32, 0:QT], psR[0:32, :], csq[0:32, :])
                p.vec("tensor_mul", ropeTq[32:64, 0:QT], psR[32:64, :], csq[32:64, :])
                p.vec("tensor_mul", ropeTq[0:32, QT:2 * QT], psR[0:32, :], csq2[0:32, :])
                p.vec("tensor_mul", ropeTq[32:64, QT:2 * QT], psR[32:64, :],
                      csq2[32:64, :], sem="sEvA")
                p.wait("tensor", "sEvA", p.cnt["sEvA"])
                for half in range(2):
                    p.mm(psR2[:, half * 512:(half + 1) * 512], mrg1[:],
                         ropeTq[:, half * 512:(half + 1) * 512], True, False)
                    p.mm(psR2[:, half * 512:(half + 1) * 512], mrg2[:],
                         ropeTq[:, QT + half * 512:QT + (half + 1) * 512], False, True,
                         sem="sPA" if half == 1 else None)
                p.wait("vector", "sPA", p.cnt["sPA"])
                p.vec("tensor_scalar", QR[:, h_ * QT:(h_ + 1) * QT], psR2[:],
                      SCALE, None, op0=OP.mult, sem="sEvA")

            p.wait("sync", "sKV", 1)
            kv_rb = p.dma("sync",
                          kRb[:, :].rearrange("q (c t) -> q c t", c=8),
                          cc_kvout.ap().rearrange("c q t -> q c t"))
            p.wait("sync", "ld", kv_rb)

            p.barrier()
            if stop_after == "A":
                return

            if debug:
                p.dma("gpsimd", taps["t_ckv"], cKV.rearrange("q a b -> q (a b)"), sem="gld")
                p.dma("gpsimd", taps["t_kr"], kRb[:, :], sem="gld")
                p.dma("gpsimd", taps["t_qc"], QC[:, :], sem="gld")
                p.dma("gpsimd", taps["t_qr"], QR[:, :], sem="gld")

            ml = p.dma("sync", maskb, d_masks)
            p.wait("vector", "ld", ml)
            p.vec("memset", ztile[:], 0.0, sem="sM")
            mask_ready = p.cnt["sM"]

            def load_head_w(h_, gate=0):
                par = h_ % 3
                p.wait("gpsimd", "sExp", gate)
                p.dma("gpsimd", wukb[par],
                      d_wuk.rearrange("(lc q) d -> q lc d", q=128)[:, :, h_ * DH:(h_ + 1) * DH],
                      sem="gld")
                return p.dma("gpsimd", wuvb[par],
                             d_wuv.rearrange("(lc q) d -> q lc d", q=128)[:, :, h_ * DH:(h_ + 1) * DH],
                             sem="gld")

            wk_ready = [0] * NH
            wk_ready[0] = load_head_w(0)
            wk_ready[1] = load_head_w(1)
            N_UNITS = 3 * 43
            g_state = {
                "loaded": 0,
                "idx_ready": [0] * N_UNITS,
                "store_v": [0, 0],
                "vec_done": 0,
            }

            def gather_prefetch():
                while (g_state["loaded"] < N_UNITS
                       and g_state["loaded"] < g_state["vec_consumed_units"] + 2):
                    k = g_state["loaded"]
                    mi, u = divmod(k, 43)
                    dsh = (d_gsh, d_ush, d_dsh)[mi]
                    slot = k % 2
                    p.wait("gpsimd", "sGat", max(0, (k - 2 + 1)))
                    g_state["idx_ready"][k] = p.dma(
                        "gpsimd", gidxb[slot][:],
                        dsh[:, u * 256:(u + 1) * 256], sem="gld")
                    g_state["loaded"] += 1

            def gather_steps():
                for k in range(N_UNITS):
                    mi, u = divmod(k, 43)
                    slot = k % 2
                    aslot = k % 2
                    def first(k=k, slot=slot, aslot=aslot):
                        p.wait("vector", "gld", g_state["idx_ready"][k])
                        p.wait("vector", "sLut", lut_ready)
                        p.wait("vector", "gld", g_state["store_v"][aslot])
                        p.vec("tensor_scalar", gacc[aslot][:], gidxb[slot][:],
                              0.0, lutb[:, 0:1], op0=OP.is_equal, op1=OP.mult)
                    yield first

                    def eqk(kk, slot=slot):
                        p.vec("tensor_scalar", gtmp[kk % 2][:], gidxb[slot][:],
                              float(kk), lutb[:, kk:kk + 1],
                              op0=OP.is_equal, op1=OP.mult)

                    def addk(kk, aslot=aslot):
                        p.vec("tensor_add", gacc[aslot][:],
                              gacc[aslot][:], gtmp[kk % 2][:])

                    yield (lambda: eqk(1))
                    for kk in range(2, 16):
                        yield (lambda kk=kk: eqk(kk))
                        yield (lambda kk=kk: addk(kk - 1))
                    yield (lambda: addk(15))
                    def fin(k=k, mi=mi, u=u, aslot=aslot):
                        gv = p.mark("vector", "sGat")
                        g_state["vec_consumed_units"] = k + 1
                        p.wait("gpsimd", "sGat", gv)
                        g_state["store_v"][aslot] = p.dma(
                            "gpsimd",
                            cc_in[mi].ap()[u * 2:u * 2 + 2, :, :].rearrange(
                                "tl q o -> q tl o"),
                            gacc[aslot][:], sem="gld")
                        gather_prefetch()
                    yield fin

            g_state["vec_consumed_units"] = 0
            g_iter = gather_steps()
            g_done = [False]
            GATHER_HEADS = GATHER_HEADS_CFG
            G_STEPS_TOTAL = N_UNITS * 32 + 1
            g_per_tile = -(-G_STEPS_TOTAL // (GATHER_HEADS * 64))

            def emit_gather_quota(nsteps):
                for _ in range(nsteps):
                    stp = next(g_iter, None)
                    if stp is None:
                        g_done[0] = True
                        return
                    stp()

            last_exp = [0, 0]
            n_iter = 0
            head_evac = 0

            gather_prefetch()
            ag_issued = [False]

            exp_done = [0] * NH

            def emit_expansion(hx):
                parw = hx % 3
                p.wait("tensor", "gld", wk_ready[hx])
                p.wait("tensor", "sP", max(last_exp))
                p.wait("tensor", "sHd", head_evac)
                p.wait("tensor", "sEvc", p.cnt["sEvc"])
                for half in range(2):
                    for lc in range(4):
                        p.mm(psSb[:, half * 512:(half + 1) * 512],
                             wukb[parw][:, lc, :],
                             cKV[:, lc, half * 512:(half + 1) * 512],
                             lc == 0, lc == 3)
                for tl in range(8):
                    for lc in range(4):
                        p.mm(psSa[:, tl * 128:(tl + 1) * 128],
                             cKV[:, lc, tl * 128:(tl + 1) * 128],
                             wuvb[parw][:, lc, :], lc == 0, lc == 3,
                             sem="sExp" if (tl == 7 and lc == 3) else None)
                exp_done[hx] = p.cnt["sExp"]
                p.wait("vector", "sExp", exp_done[hx])
                p.vec("tensor_copy", kstage[:, 0:1024], psSb[:])
                p.vec("tensor_copy", kstage[:, 1024:2048], psSa[:], sem="sEvc")
                evc = p.cnt["sEvc"]
                p.wait("gpsimd", "sEvc", evc)
                if hx >= 2:
                    p.wait("gpsimd", "sKX", hx - 1)
                p.dma("gpsimd", cc_xin[hx % 2].ap()[0], kstage[:, 0:1024],
                      sem="gld")
                stv = p.dma("gpsimd", cc_xin[hx % 2].ap()[1],
                            kstage[:, 1024:2048], sem="gld")
                p.wait("gpsimd", "gld", stv)
                if hx >= 2:
                    p.wait("gpsimd", "sF", (hx - 1) * 64)
                inst = None
                if p.on("gpsimd"):
                    inst = p.eng.collective_compute(
                        "AllGather", OP.bypass,
                        replica_groups=[list(range(NCORE))],
                        ins=[cc_xin[hx % 2].ap().opt()],
                        outs=[cc_xout[hx % 2].ap().opt()])
                p.inc("sKX", 1, inst)
                return evc

            kt_free = [0, 0]
            kt_ready = {}

            def load_kt(k):
                h, tc8 = divmod(k, 8)
                slot = k % 2
                p.wait("sync", "sKX", h + 1)
                p.wait("sync", "sF", kt_free[slot])
                p.dma("sync", KCb[slot][:], cc_xout[h % 2].ap()[2 * tc8])
                kt_ready[(h, tc8)] = p.dma("sync", Vb[slot][:],
                                           cc_xout[h % 2].ap()[2 * tc8 + 1])
                kt_free[slot] = h * 64 + (tc8 + 1) * 8

            def maybe_issue_ag(h=None):
                if ag_issued[0] or not g_done[0]:
                    return
                if h is not None and h < AG_MIN_HEAD:
                    return
                ag_issued[0] = True
                p.wait("gpsimd", "gld", p.cnt["gld"])
                for mi in range(3):
                    inst = None
                    if p.on("gpsimd"):
                        inst = p.eng.collective_compute(
                            "AllGather", OP.bypass,
                            replica_groups=[list(range(NCORE))],
                            ins=[cc_in[mi].ap().opt()],
                            outs=[cc_out[mi].ap().opt()])
                    p.inc("sAG", 1, inst)

            NHL = nh_lim or NH
            emit_expansion(0)
            load_kt(0)
            load_kt(1)

            for h_ in range(NHL):
                if h_ + 2 < NHL:
                    wk_ready[h_ + 2] = load_head_w(
                        h_ + 2, gate=exp_done[h_ - 1] if h_ >= 1 else 0)
                if h_ + 1 < NHL:
                    evc_gate = emit_expansion(h_ + 1)
                else:
                    evc_gate = p.cnt["sEvc"]

                for tc8 in range(8):
                    kt = h_ * 8 + tc8
                    if kt + 2 < NHL * 8:
                        load_kt(kt + 2)
                    kslot = kt % 2
                    p.wait("tensor", "ld", kt_ready[(h_, tc8)])
                    p.wait("tensor", "sEvc", evc_gate)
                    if tc8 == 0 and h_ > 0:
                        p.wait("tensor", "sHd", head_evac)

                    for tp_ in range(4):
                        pair = []
                        for tl in (tp_ * 2, tp_ * 2 + 1):
                            t_ = tc8 * 8 + tl
                            n = n_iter
                            n_iter += 1
                            jmin = t_ // 8
                            c0 = jmin * 128
                            par = n % 2
                            psS = psSa if par == 0 else psSb
                            segs = []
                            lo = c0
                            while lo < QT:
                                hi = min((lo // 512 + 1) * 512, QT)
                                segs.append((lo, hi))
                                lo = hi
                            p.wait("tensor", "sP", last_exp[par])
                            qb = h_ * QT
                            for si, (lo, hi) in enumerate(segs):
                                p.mm(psS[:, lo:hi],
                                     KCb[kslot][:, tl * 128:(tl + 1) * 128],
                                     QC[:, qb + lo:qb + hi], True, False)
                                p.mm(psS[:, lo:hi], kRb[:, t_ * 128:(t_ + 1) * 128],
                                     QR[:, qb + lo:qb + hi], False, True,
                                     sem="sS" if si == len(segs) - 1 else None)
                            sv = p.cnt["sS"]
                            pb = probs[n % 3]
                            p.wait("scalar", "sS", sv)
                            if h_ == 0 and t_ == 0:
                                p.wait("scalar", "sLut", lut_ready)
                            if n >= 3:
                                p.wait("scalar", "sF", n - 2)
                            p.act(pb[:, c0:QT], psS[:, c0:QT], AF.Exp, bias=ebias[:],
                                  sem="sP")
                            last_exp[par] = p.cnt["sP"]
                            p.wait("vector", "sP", p.cnt["sP"])
                            p.wait("vector", "sM", mask_ready)
                            p.vec("copy_predicated", pb[:, c0:c0 + 128],
                                  maskb[:, t_ * 128:(t_ + 1) * 128], ztile[:], sem="sM")
                            pair.append((t_, tl, segs, pb, p.cnt["sM"]))
                            if h_ < GATHER_HEADS and not g_done[0]:
                                emit_gather_quota(g_per_tile)
                        for (t_, tl, segs, pb, mv) in pair:
                            p.wait("tensor", "sM", mv)
                            first = (t_ == 0)
                            last = (t_ == NKT - 1)
                            for (lo, hi) in segs:
                                p.mm(psSum[:, lo:hi], onesb[:], pb[:, lo:hi], first, last)
                            for (lo, hi) in segs:
                                p.mm(psO[:, lo:hi],
                                     Vb[kslot][:, tl * 128:(tl + 1) * 128],
                                     pb[:, lo:hi], first, last)
                            p.inc("sF", 1, p.eng.nop() if p.on("tensor") else None)

                maybe_issue_ag(h_)
                p.wait("vector", "sF", p.cnt["sF"])
                p.vec("reciprocal", rn[:], psSum[:])
                rv = p.mark("vector", "sHd")
                p.wait("tensor", "sHd", rv)
                p.wait("tensor", "sP", max(last_exp))
                for half in range(2):
                    p.mm(psSa[:, half * 512:(half + 1) * 512], onesrf[:],
                         rn[:, half * 512:(half + 1) * 512], True, True,
                         sem="sHd" if half == 1 else None)
                bv = p.cnt["sHd"]
                p.wait("vector", "sHd", bv)
                for qq in range(4):
                    p.vec("tensor_copy", rnb[:], psSa[:, qq * 256:(qq + 1) * 256])
                    p.vec("tensor_mul",
                          o_all[:, h_ * QT + qq * 256:h_ * QT + (qq + 1) * 256],
                          psO[:, qq * 256:(qq + 1) * 256], rnb[:],
                          sem="sHd" if qq == 3 else None)
                head_evac = p.cnt["sHd"]
                last_exp = [p.cnt["sP"], p.cnt["sP"]]

            p.barrier()
            if stop_after == "B":
                return

            if debug:
                p.dma("gpsimd", taps["t_oall"], o_all[:, :], sem="gld")

            psT_last = [0, 0]
            hbufC_free = 0
            for qt in range(8):
                p.wait("sync", "sS", hbufC_free)
                lv = p.dma("sync", hbufC[:], d_hq[qt * 128:(qt + 1) * 128, :])
                p.wait("tensor", "ld", lv)
                for dc in range(16):
                    i = dc % 2
                    p.wait("tensor", "sZ", psT_last[i])
                    p.tr(psT[i][:], hbufC[:, dc * 128:(dc + 1) * 128], identf[:],
                         sem="sS")
                    p.wait("vector", "sS", p.cnt["sS"])
                    zv = p.vec("tensor_copy", z[:, dc, qt * 128:(qt + 1) * 128],
                               psT[i][:], sem="sZ")
                    psT_last[i] = p.cnt["sZ"]
                hbufC_free = p.cnt["sS"]
            zinit = p.cnt["sZ"]
            if debug:
                p.wait("gpsimd", "sZ", zinit)
                p.dma("gpsimd", taps["t_z1"], z.rearrange("q a b -> q (a b)"), sem="gld")

            wo_view = d_wo.rearrange("(dcI q) n -> q dcI n", q=128)
            wo_ready = [0] * 16
            wo_free = [0, 0]
            wo_ready[0] = p.dma("gpsimd", wotile[0], wo_view[:, :, 0:128], sem="gld")
            for dcO in range(16):
                par = dcO % 2
                if dcO + 1 < 16:
                    p.wait("gpsimd", "sS", wo_free[(dcO + 1) % 2])
                    wo_ready[dcO + 1] = p.dma(
                        "gpsimd", wotile[(dcO + 1) % 2],
                        wo_view[:, :, (dcO + 1) * 128:(dcO + 2) * 128], sem="gld")
                p.wait("tensor", "gld", wo_ready[dcO])
                if dcO >= 2:
                    p.wait("tensor", "sZ", zinit + dcO - 1)
                for half in range(2):
                    for dcI in range(16):
                        p.mm(psW[par][:, half * 512:(half + 1) * 512],
                             wotile[par][:, dcI, :],
                             o_all[:, dcI * QT + half * 512:dcI * QT + (half + 1) * 512],
                             dcI == 0, dcI == 15,
                             sem="sS" if (dcI == 15 and half == 1) else None)
                wo_free[par] = p.cnt["sS"]
                p.wait("vector", "sS", p.cnt["sS"])
                p.wait("vector", "sZ", zinit)
                p.vec("tensor_add", z[:, dcO, :], z[:, dcO, :], psW[par][:], sem="sZ")
            zdone = p.cnt["sZ"]
            if debug:
                p.wait("gpsimd", "sZ", zdone)
                p.dma("gpsimd", taps["t_z2"], z.rearrange("q a b -> q (a b)"), sem="gld")

            sq_free = 0
            for dc in range(16):
                p.wait("scalar", "sZ", zinit + 1 + dc)
                p.wait("scalar", "sL2", sq_free)
                p.act(sqz[:], z[:, dc, :], AF.Square, sem="sSq")
                p.wait("tensor", "sSq", p.cnt["sSq"])
                p.wait("tensor", "sZ", zinit + 1 + dc)
                for half in range(2):
                    p.mm(psW[0][0:1, half * 512:(half + 1) * 512], onesf[:],
                         z[:, dc, half * 512:(half + 1) * 512], dc == 0, dc == 15)
                    p.mm(psW[1][0:1, half * 512:(half + 1) * 512], onesf[:],
                         sqz[:, half * 512:(half + 1) * 512], dc == 0, dc == 15,
                         sem="sL2" if half == 1 else None)
                sq_free = p.cnt["sL2"]
            p.wait("vector", "sL2", p.cnt["sL2"])
            mu_ = l2s[:, 0:QT]
            var_ = l2s[:, QT:2 * QT]
            tmp_ = sqz[0:1, 0:QT]
            isd_ = isdt[0:1, :]
            p.vec("tensor_scalar", mu_, psW[0][0:1, :], 1.0 / H, None, op0=OP.mult)
            mv = p.mark("vector", "sL2")
            p.vec("tensor_scalar", var_, psW[1][0:1, :], 1.0 / H, EPS,
                  op0=OP.mult, op1=OP.add)
            s1v = p.mark("vector", "sL2")
            p.wait("scalar", "sL2", mv)
            p.act(tmp_, mu_, AF.Square, sem="sSq")
            p.wait("vector", "sSq", p.cnt["sSq"])
            p.wait("vector", "sL2", s1v)
            p.vec("tensor_sub", var_, var_, tmp_)
            vv = p.mark("vector", "sL2")
            p.wait("scalar", "sL2", vv)
            p.act(var_, var_, AF.Sqrt, sem="sSq")
            p.wait("vector", "sSq", p.cnt["sSq"])
            p.vec("reciprocal", isd_, var_)
            rv = p.mark("vector", "sL2")
            p.wait("tensor", "sL2", rv)
            for half in range(2):
                p.mm(psW[0][:, half * 512:(half + 1) * 512], onesrf[:],
                     mu_[:, half * 512:(half + 1) * 512], True, True)
                p.mm(psW[1][:, half * 512:(half + 1) * 512], onesrf[:],
                     isd_[:, half * 512:(half + 1) * 512], True, True,
                     sem="sL2" if half == 1 else None)
            p.wait("vector", "sL2", p.cnt["sL2"])
            for dc in range(16):
                p.vec("tensor_sub", sqz[:], z[:, dc, :], psW[0][:])
                p.vec("tensor_mul", zln[:, dc, :], sqz[:], psW[1][:],
                      sem="sL2" if dc == 15 else None)
            zln_done = p.cnt["sL2"]
            if debug:
                p.wait("gpsimd", "sL2", zln_done)
                p.dma("gpsimd", taps["t_zln"], zln.rearrange("q a b -> q (a b)"), sem="gld")

            wo_all_done = p.cnt["sS"]
            p.wait("sync", "sS", wo_all_done)
            p.wait("sync", "sAG", 3)
            ccg = cc_out[0].ap().rearrange("(hc oc) q o -> hc oc q o", hc=16)
            if debug:
                p.wait("gpsimd", "sS", wo_all_done)
                p.wait("gpsimd", "sAG", 3)
                p.dma("gpsimd", taps["t_ccge"],
                      cc_out[0].ap().rearrange("(hc oc) q o -> hc oc q o", hc=16)[:, 2],
                      sem="gld")
            ccu = cc_out[1].ap().rearrange("(hc oc) q o -> hc oc q o", hc=16)
            ccd_ = cc_out[2].ap()
            gu_free = [0, 0]
            mult_done = {}
            silu_free = 0
            for qh in range(2):
                q0 = qh * 512
                gu_ready = [0] * 43
                p.wait("sync", "sFF", gu_free[0])
                p.dma("sync", wut[0], ccu[:, 0].rearrange("hc q o -> q hc o"))
                gu_ready[0] = p.dma("sync", wgt[0], ccg[:, 0].rearrange("hc q o -> q hc o"))
                for oc in range(43):
                    par = oc % 2
                    if oc + 1 < 43:
                        p.wait("sync", "sFF", gu_free[(oc + 1) % 2])
                        p.dma("sync", wut[(oc + 1) % 2],
                              ccu[:, oc + 1].rearrange("hc q o -> q hc o"))
                        gu_ready[oc + 1] = p.dma(
                            "sync", wgt[(oc + 1) % 2],
                            ccg[:, oc + 1].rearrange("hc q o -> q hc o"))
                    if debug and qh == 0 and 2 <= oc <= 5:
                        p.wait("gpsimd", "ld", gu_ready[oc])
                        p.dma("gpsimd",
                              taps["t_wgt"][:, (oc - 2) * 2048:(oc - 1) * 2048],
                              wgt[par].rearrange("q a b -> q (a b)"), sem="gld")
                        tv = p.dma("gpsimd",
                                   taps["t_wut"][:, (oc - 2) * 2048:(oc - 1) * 2048],
                                   wut[par].rearrange("q a b -> q (a b)"), sem="gld")
                        p.wait("tensor", "gld", tv)
                    p.wait("tensor", "ld", gu_ready[oc])
                    p.wait("tensor", "sL2", zln_done)
                    key = (qh, oc - 2)
                    if key in mult_done:
                        p.wait("tensor", "sFF", mult_done[key])
                    elif qh > 0 and oc < 2:
                        p.wait("tensor", "sFF", mult_done[(0, 41 + oc)])
                    p.wait("tensor", "sP", silu_free)
                    for hc in range(16):
                        p.mm(psW[0][:, par * 512:par * 512 + 512],
                             wgt[par][:, hc, :], zln[:, hc, q0:q0 + 512],
                             hc == 0, hc == 15)
                    for hc in range(16):
                        p.mm(psW[1][:, par * 512:par * 512 + 512],
                             wut[par][:, hc, :], zln[:, hc, q0:q0 + 512],
                             hc == 0, hc == 15,
                             sem="sFF" if hc == 15 else None)
                    gu_free[par] = p.cnt["sFF"]
                    gv = p.cnt["sFF"]
                    if debug and qh == 0 and oc == 2:
                        p.wait("scalar", "sFF", gv)
                        p.act(sqz[:, 0:512],
                              psW[0][:, par * 512:par * 512 + 512], AF.Identity)
                        p.act(sqz[:, 512:1024],
                              psW[1][:, par * 512:par * 512 + 512], AF.Identity,
                              sem="sFF")
                        p.wait("gpsimd", "sFF", p.cnt["sFF"])
                        p.dma("gpsimd", taps["t_psg"], sqz[:, 0:512], sem="gld")
                        ps_tap_v = p.dma("gpsimd", taps["t_psu"],
                                         sqz[:, 512:1024], sem="gld")
                    if debug and qh == 0 and oc == 4:
                        p.wait("tensor", "gld", ps_tap_v)
                    p.wait("scalar", "sFF", gv)
                    p.act(gsl[:], psW[0][:, par * 512:par * 512 + 512], AF.Silu,
                          sem="sP")
                    silu_free = p.cnt["sP"]
                    p.wait("vector", "sP", p.cnt["sP"])
                    p.vec("tensor_mul", actb[:, oc, :], gsl[:],
                          psW[1][:, par * 512:par * 512 + 512], sem="sFF")
                    mult_done[(qh, oc)] = p.cnt["sFF"]
                if debug and qh == 0:
                    p.wait("gpsimd", "sFF", mult_done[(0, 42)])
                    tv = p.dma("gpsimd", taps["t_act"], actb.rearrange("q a b -> q (a b)"), sem="gld")
                    p.wait("tensor", "gld", tv)
                wd_ready = [0] * 16
                wd_free = [0, 0]
                wd_ready[0] = p.dma("sync", wdt[0],
                                    ccd_[0:43].rearrange("tl q o -> q tl o"))
                for dcO in range(16):
                    par = dcO % 2
                    if dcO + 1 < 16:
                        p.wait("sync", "sFF", wd_free[(dcO + 1) % 2])
                        wd_ready[dcO + 1] = p.dma(
                            "sync", wdt[(dcO + 1) % 2],
                            ccd_[(dcO + 1) * 43:(dcO + 2) * 43].rearrange("tl q o -> q tl o"))
                    p.wait("tensor", "ld", wd_ready[dcO])
                    p.wait("tensor", "sFF", p.cnt["sFF"])
                    for oc in range(43):
                        p.mm(ps3[:], wdt[par][:, oc, :], actb[:, oc, :],
                             oc == 0, oc == 42, sem="sFF" if oc == 42 else None)
                    wd_free[par] = p.cnt["sFF"]
                    p.wait("vector", "sFF", p.cnt["sFF"])
                    p.vec("tensor_add", z[:, dcO, q0:q0 + 512],
                          z[:, dcO, q0:q0 + 512], ps3[:], sem="sFF")

            ffn_done = p.cnt["sFF"]
            if debug:
                p.wait("gpsimd", "sFF", ffn_done)
                p.dma("gpsimd", taps["t_z3"], z.rearrange("q a b -> q (a b)"), sem="gld")
                p.dma("gpsimd", taps["t_oall2"], o_all[:, :], sem="gld")
                p.dma("gpsimd", taps["t_ccg"], cc_out[0].ap()[0:8], sem="gld")
                p.dma("gpsimd", taps["t_ccd"], cc_out[2].ap()[0:8], sem="gld")
            p.wait("tensor", "sFF", ffn_done)
            psT_last = [p.cnt["sZ"], p.cnt["sZ"]]
            outb_free = [0, 0]
            for qt in range(8):
                par = qt % 2
                for dc in range(16):
                    i = dc % 2
                    p.wait("tensor", "sZ", psT_last[i])
                    p.tr(psT[i][:], z[:, dc, qt * 128:(qt + 1) * 128], identf[:],
                         sem="sS")
                    p.wait("vector", "sS", p.cnt["sS"])
                    p.wait("vector", "sOut", outb_free[par])
                    p.vec("tensor_copy", outb[par][:, dc * 128:(dc + 1) * 128],
                          psT[i][:], sem="sZ")
                    psT_last[i] = p.cnt["sZ"]
                p.wait("gpsimd", "sZ", p.cnt["sZ"])
                sv = p.dma("gpsimd", d_out[qt * 128:(qt + 1) * 128, :], outb[par][:],
                           sem="sOut")
                outb_free[par] = sv
            p.wait("gpsimd", "sOut", p.cnt["sOut"])

        with nc.Block() as block:
            @block.sync
            def _(eng):
                p.E, p.eng = "sync", eng
                p.reset()
                walk()

            @block.gpsimd
            def _(eng):
                p.E, p.eng = "gpsimd", eng
                p.reset()
                walk()

            @block.tensor
            def _(eng):
                p.E, p.eng = "tensor", eng
                p.reset()
                walk()

            @block.vector
            def _(eng):
                p.E, p.eng = "vector", eng
                p.reset()
                walk()

            @block.scalar
            def _(eng):
                p.E, p.eng = "scalar", eng
                p.reset()
                walk()

    return nc


def _host_inputs(inputs):
    h = np.ascontiguousarray(np.asarray(inputs["L0_attn_h"]).reshape(S, H))
    lut = np.ascontiguousarray(np.asarray(inputs["L0_ffn_lut"]).reshape(1, 16))

    def idx8(a):
        a = np.ascontiguousarray(np.asarray(a)).view(np.int8)
        return a.reshape(a.shape[0], -1, 4)[:, :, 0]

    gidx = idx8(inputs["L0_ffn_gate_idx"])
    uidx = idx8(inputs["L0_ffn_up_idx"])
    didx = idx8(inputs["L0_ffn_down_idx"])

    def tiles_gu(a):
        t = np.ascontiguousarray(a.T)
        t = t.reshape(16, 128, 43, 128)
        return np.ascontiguousarray(t.transpose(0, 2, 1, 3).reshape(NTILE, 128, 128))

    def tiles_d(a):
        t = np.ascontiguousarray(a.T)
        t = t.reshape(43, 128, 16, 128)
        return np.ascontiguousarray(t.transpose(2, 0, 1, 3).reshape(NTILE, 128, 128))

    gt = tiles_gu(gidx)
    ut = tiles_gu(uidx)
    dt_ = tiles_d(didx)

    theta = 1.0 / 10000.0 ** (np.arange(0, RD, 2, dtype=np.float64) / RD)
    perm = np.concatenate([np.arange(0, RD, 2), np.arange(1, RD, 2)])
    wkr = np.ascontiguousarray(np.asarray(inputs["L0_attn_W_KR"])[:, perm])
    wqr_ = np.asarray(inputs["L0_attn_W_QR"]).reshape(LL, NH, RD)
    wqr = np.ascontiguousarray(wqr_[:, :, perm].reshape(LL, NH * RD))

    shared = {
        "wdkv": np.ascontiguousarray(np.asarray(inputs["L0_attn_W_DKV"])),
        "wuk": np.ascontiguousarray(np.asarray(inputs["L0_attn_W_UK"])),
        "wkr": wkr,
        "wuv": np.ascontiguousarray(np.asarray(inputs["L0_attn_W_UV"])),
        "wdq": np.ascontiguousarray(np.asarray(inputs["L0_attn_W_DQ"])),
        "wuq": np.ascontiguousarray(np.asarray(inputs["L0_attn_W_UQ"])),
        "wqr": wqr,
        "wo": np.ascontiguousarray(np.asarray(inputs["L0_attn_W_O"])),
        "lut16": lut,
    }

    in_maps, all_pos = [], []
    for c in range(NCORE):
        chunks = [c + 8 * j for j in range(8)]
        posn = np.concatenate([np.arange(g * 128, (g + 1) * 128) for g in chunks])
        all_pos.append(posn)

        def shard(tl):
            ss = tl[SH_T * c:SH_T * (c + 1)]
            return np.ascontiguousarray(ss.transpose(1, 0, 2).reshape(128, SH_T * 128))

        m = np.zeros((NKT, 128, 128), dtype=np.uint8)
        ii = np.arange(128)
        for t in range(NKT):
            jmin = t // 8
            g = c + 8 * jmin
            if t > g:
                m[t, :, :] = 1
            elif t == g:
                m[t] = (ii[:, None] > ii[None, :]).astype(np.uint8)
        masks = np.ascontiguousarray(m.transpose(1, 0, 2).reshape(128, NKT * 128))
        ang = posn[None, :].astype(np.float64) * theta[:, None]
        csq = np.concatenate([np.cos(ang), np.sin(ang)], axis=0).astype(np.float32)
        csq2 = np.concatenate([np.sin(ang), np.cos(ang)], axis=0).astype(np.float32)
        posk = np.arange(QT * c, QT * (c + 1), dtype=np.float64)
        angk = posk[None, :] * theta[:, None]
        csk = np.concatenate([np.cos(angk), np.sin(angk)], axis=0)
        csk2 = np.concatenate([np.sin(angk), np.cos(angk)], axis=0)
        mp = dict(shared)
        mp.update({
            "hq": np.ascontiguousarray(h[posn]),
            "hkv": np.ascontiguousarray(h[QT * c:QT * (c + 1)]),
            "gsh": shard(gt), "ush": shard(ut), "dsh": shard(dt_),
            "masks": masks,
            "cosq": np.ascontiguousarray(csq),
            "cosq2": np.ascontiguousarray(csq2),
            "coskv": csk.astype(ml_dtypes.bfloat16),
            "coskv2": csk2.astype(ml_dtypes.bfloat16),
        })
        in_maps.append(mp)
    return in_maps, all_pos


_PROGRAM_CACHE = {}


def kernel(**inputs):
    in_maps, all_pos = _host_inputs(inputs)
    if "nc" not in _PROGRAM_CACHE:
        _PROGRAM_CACHE["nc"] = build_program()
    nc = _PROGRAM_CACHE["nc"]
    res = run_bass_kernel_spmd(nc, in_maps, core_ids=list(range(NCORE)))
    out = np.empty((S, H), dtype=np.float32)
    for c in range(NCORE):
        out[all_pos[c]] = res.results[c]["out"]
    return out.reshape(1, S, H)


# version 07
# speedup vs baseline: 1.0479x; 1.0479x over previous
import numpy as np
import ml_dtypes
from contextlib import ExitStack

import concourse.bass as bass
import concourse.mybir as mybir
from concourse.bass_utils import run_bass_kernel_spmd

F32 = mybir.dt.float32
BF16 = mybir.dt.bfloat16
I8 = mybir.dt.int8
U8 = mybir.dt.uint8
AF = mybir.ActivationFunctionType
OP = mybir.AluOpType
AXX = mybir.AxisListType.X

DEFER_PV = False
DEFER_FROM_HEAD = 0
GATHER_HEADS_CFG = 12
AG_MIN_HEAD = 0
H = 2048
LL = 512
RD = 64
NH = 16
DH = 128
S = 8192
QT = 1024
INTER = 5504
NCORE = 8
NKT = 64
SCALE = 1.0 / float(np.sqrt(DH + RD))
EXP_BIAS = -20.0
EPS = 1e-5
NTILE = 688
SH_T = NTILE // NCORE
GU_T = 43
GU_F = GU_T * 128


class P:
    def __init__(self, nc):
        self.nc = nc
        self.sems = {}
        self.cnt = {}
        self.E = None
        self.eng = None
        self.last_inst = None
        self.last_inst_inc = None

    def mksem(self, ctx, names):
        for n in names:
            self.sems[n] = ctx.enter_context(self.nc.semaphore(n))
            self.cnt[n] = 0

    def reset(self):
        self.last_inst = None
        self.last_inst_inc = None
        for k in self.cnt:
            self.cnt[k] = 0

    def on(self, name):
        return self.E == name

    def inc(self, sem, n, inst):
        self.cnt[sem] += n
        if inst is not None:
            inst.then_inc(self.sems[sem], n)
            if inst is self.last_inst:
                self.last_inst_inc = (sem, self.cnt[sem])
        return self.cnt[sem]

    def wait(self, eng, sem, val):
        if val <= 0:
            return
        if self.on(eng):
            self.eng.wait_ge(self.sems[sem], val)

    def _completion_anchor(self):
        if self.last_inst is None:
            return self.eng.nop()
        if self.last_inst_inc is None:
            return self.last_inst
        sem, val = self.last_inst_inc
        self.eng.wait_ge(self.sems[sem], val)
        ret = self.eng.nop()
        self.last_inst = None
        return ret

    def mark(self, eng, sem):
        inst = self._completion_anchor() if self.on(eng) else None
        return self.inc(sem, 1, inst)

    def dma(self, queue, out, in_, sem="ld", **kw):
        inst = None
        if self.on(queue):
            inst = self.eng.dma_start(out=out, in_=in_, **kw)
            self.last_inst = inst
            self.last_inst_inc = None
        return self.inc(sem, 16, inst)

    def dmaT(self, queue, out, in_, sem="tp"):
        inst = None
        if self.on(queue):
            inst = self.eng.dma_start(out=out, in_=in_, transpose=True)
        return self.inc(sem, 16, inst)

    def mm(self, out, lhsT, rhs, start, stop, sem=None):
        inst = None
        if self.on("tensor"):
            inst = self.nc.tensor.matmul(out, lhsT, rhs, start=start, stop=stop)
            self.last_inst = inst
            self.last_inst_inc = None
        if sem is not None:
            return self.inc(sem, 1, inst)
        return None

    def tr(self, out, in_, ident, sem=None):
        inst = None
        if self.on("tensor"):
            inst = self.nc.tensor.transpose(out, in_, ident)
            self.last_inst = inst
            self.last_inst_inc = None
        if sem is not None:
            return self.inc(sem, 1, inst)
        return None

    def act(self, out, in_, func, sem=None, **kw):
        inst = None
        if self.on("scalar"):
            inst = self.nc.scalar.activation(out, in_, func, **kw)
            self.last_inst = inst
            self.last_inst_inc = None
        if sem is not None:
            return self.inc(sem, 1, inst)
        return None

    def vec(self, meth, *args, sem=None, **kw):
        inst = None
        if self.on("vector"):
            inst = getattr(self.nc.vector, meth)(*args, **kw)
            self.last_inst = inst
            self.last_inst_inc = None
        if sem is not None:
            return self.inc(sem, 1, inst)
        return None

    def barrier(self):
        base = self.cnt["bar"]
        if self.eng is not None:
            if self.E in ("tensor", "vector", "scalar"):
                inst = self._completion_anchor()
            else:
                inst = self.eng.nop()
            inst.then_inc(self.sems["bar"], 1)
        self.cnt["bar"] = base + 5
        if self.eng is not None:
            self.eng.wait_ge(self.sems["bar"], base + 5)


def build_program(debug=False, stop_after=None, nh_lim=None):
    nc = bass.Bass("TRN2", target_bir_lowering=False, num_devices=NCORE)

    d_hkv = nc.dram_tensor("hkv", [QT, H], F32, kind="ExternalInput").ap()
    d_hq = nc.dram_tensor("hq", [QT, H], F32, kind="ExternalInput").ap()
    d_coskv = nc.dram_tensor("coskv", [64, QT], BF16, kind="ExternalInput").ap()
    d_coskv2 = nc.dram_tensor("coskv2", [64, QT], BF16, kind="ExternalInput").ap()
    d_wdkv = nc.dram_tensor("wdkv", [H, LL], F32, kind="ExternalInput").ap()
    d_wuk = nc.dram_tensor("wuk", [LL, H], F32, kind="ExternalInput").ap()
    d_wkr = nc.dram_tensor("wkr", [H, RD], F32, kind="ExternalInput").ap()
    d_wuv = nc.dram_tensor("wuv", [LL, H], F32, kind="ExternalInput").ap()
    d_wdq = nc.dram_tensor("wdq", [H, LL], F32, kind="ExternalInput").ap()
    d_wuq = nc.dram_tensor("wuq", [LL, H], F32, kind="ExternalInput").ap()
    d_wqr = nc.dram_tensor("wqr", [LL, NH * RD], F32, kind="ExternalInput").ap()
    d_wo = nc.dram_tensor("wo", [H, H], F32, kind="ExternalInput").ap()
    d_lut = nc.dram_tensor("lut16", [1, 16], F32, kind="ExternalInput").ap()
    d_gsh = nc.dram_tensor("gsh", [128, SH_T * 128], I8, kind="ExternalInput").ap()
    d_ush = nc.dram_tensor("ush", [128, SH_T * 128], I8, kind="ExternalInput").ap()
    d_dsh = nc.dram_tensor("dsh", [128, SH_T * 128], I8, kind="ExternalInput").ap()
    d_masks = nc.dram_tensor("masks", [128, NKT * 128], U8, kind="ExternalInput").ap()
    d_cosq = nc.dram_tensor("cosq", [64, QT], F32, kind="ExternalInput").ap()
    d_cosq2 = nc.dram_tensor("cosq2", [64, QT], F32, kind="ExternalInput").ap()
    d_out = nc.dram_tensor("out", [QT, H], F32, kind="ExternalOutput").ap()
    taps = {}
    if debug:
        for nm, sh, dt in [("t_ckv", [128, 4 * QT], BF16), ("t_kr", [64, S], BF16),
                           ("t_qc", [128, NH * QT], BF16), ("t_qr", [64, NH * QT], BF16),
                           ("t_oall", [128, NH * QT], BF16),
                           ("t_z1", [128, 16 * QT], F32), ("t_z2", [128, 16 * QT], F32),
                           ("t_zln", [128, 16 * QT], BF16), ("t_act", [128, 43 * 512], BF16),
                           ("t_z3", [128, 16 * QT], F32),
                           ("t_oall2", [128, NH * QT], BF16),
                           ("t_ccg", [8, 128, 128], BF16),
                           ("t_ccd", [8, 128, 128], BF16),
                           ("t_ccge", [16, 128, 128], BF16),
                           ("t_wgt", [128, 4 * 16 * 128], BF16),
                           ("t_wut", [128, 4 * 16 * 128], BF16),
                           ("t_psg", [128, 512], F32),
                           ("t_psu", [128, 512], F32)]:
            taps[nm] = nc.dram_tensor(nm, sh, dt, kind="ExternalOutput").ap()

    cc_in = [nc.dram_tensor(f"cc_in_{m}", [SH_T, 128, 128], BF16) for m in "gud"]
    cc_out = [nc.dram_tensor(f"cc_out_{m}", [NTILE, 128, 128], BF16) for m in "gud"]
    cc_kvin = nc.dram_tensor("cc_kvin", [64, QT], BF16)
    cc_kvout = nc.dram_tensor("cc_kvout", [8, 64, QT], BF16)
    cc_xin = [nc.dram_tensor(f"cc_xin{i}", [2, 128, QT], BF16) for i in range(2)]
    cc_xout = [nc.dram_tensor(f"cc_xout{i}", [16, 128, QT], BF16) for i in range(2)]

    m1 = np.zeros((64, 64), dtype=np.float64)
    m2 = np.zeros((64, 64), dtype=np.float64)
    for j in range(32):
        m1[j, j] = 1.0
        m1[32 + j, j] = -1.0
        m2[j, 32 + j] = 1.0
        m2[32 + j, 32 + j] = 1.0
    i_m1 = nc.inline_tensor(m1.astype(ml_dtypes.bfloat16), "ropeM1")
    i_m2 = nc.inline_tensor(m2.astype(ml_dtypes.bfloat16), "ropeM2")
    i_identf = nc.inline_tensor(np.eye(128, dtype=np.float32), "identf")
    i_ones_bf = nc.inline_tensor(np.ones((128, 1), dtype=ml_dtypes.bfloat16), "ones_bf")
    i_ones_f = nc.inline_tensor(np.ones((128, 1), dtype=np.float32), "ones_f")
    i_onesrow_f = nc.inline_tensor(np.ones((1, 128), dtype=np.float32), "onesrow_f")

    p = P(nc)

    with ExitStack() as ctx:
        p.mksem(ctx, [
            "ld", "gld", "tp", "sLN", "sSq", "sPA", "sEvA", "sLut", "sGat",
            "sAG", "sKV", "sKX", "sExp", "sEvc", "sS", "sP", "sM", "sF", "sHd",
            "sZ", "sL2", "sFF", "sOut", "bar",
        ])

        A = lambda nm, sh, dt: ctx.enter_context(nc.sbuf_tensor(nm, sh, dt)).ap()
        lutsb = A("lutsb", [1, 16], F32)
        lutb = A("lutb", [128, 16], F32)
        onesb = A("onesb", [128, 1], BF16)
        onesf = A("onesf", [128, 1], F32)
        onesrf = A("onesrf", [1, 128], F32)
        identf = A("identf_s", [128, 128], F32)
        ebias = A("ebias", [128, 1], F32)
        mrg1 = A("mrg1", [64, 64], BF16)
        mrg2 = A("mrg2", [64, 64], BF16)
        arL1 = nc.auto_arena()
        cKV = arL1.sbuf_tensor("cKV", [128, 4, QT], BF16).ap()
        kRb = arL1.sbuf_tensor("kRb", [64, S], BF16).ap()
        arO = nc.auto_arena()
        o_all = arO.sbuf_tensor("o_all", [128, NH * QT], BF16).ap()

        arA = nc.auto_arena()
        hbuf = [arA.sbuf_tensor(f"hbuf{i}", [128, H], F32).ap() for i in range(2)]
        hln = [arA.sbuf_tensor(f"hln{i}", [128, H], BF16).ap() for i in range(2)]
        hlnT = arA.sbuf_tensor("hlnT", [128, 16, 256], BF16).ap()
        stat = arA.sbuf_tensor("stat", [128, 12], F32).ap()
        ropeT = arA.sbuf_tensor("ropeT", [64, 512], BF16).ap()
        arA1 = nc.auto_arena()
        wdkvb = arA1.sbuf_tensor("wdkvb", [128, 16, LL], BF16).ap()
        wkrb = arA1.sbuf_tensor("wkrb", [128, 16, RD], BF16).ap()
        cosks = arA1.sbuf_tensor("cosks", [64, QT], BF16).ap()
        cosks2 = arA1.sbuf_tensor("cosks2", [64, QT], BF16).ap()
        wdqb = arA1.sbuf_tensor("wdqb", [128, 16, LL], BF16).ap()
        arL2 = nc.auto_arena()
        QC = arL2.sbuf_tensor("QC", [128, NH * QT], BF16).ap()
        QR = arL2.sbuf_tensor("QR", [64, NH * QT], BF16).ap()
        KCb = [arL2.sbuf_tensor(f"KCb{i}", [128, 1024], BF16).ap() for i in range(2)]
        Vb = [arL2.sbuf_tensor(f"Vb{i}", [128, 1024], BF16).ap() for i in range(2)]
        kstage = arL2.sbuf_tensor("kstage", [128, 2048], BF16).ap()
        arA2b = nc.auto_arena()
        wuqb = arA2b.sbuf_tensor("wuqb", [128, 4, H], BF16).ap()
        wqrb = arA2b.sbuf_tensor("wqrb", [128, 4, NH * RD], BF16).ap()
        csq = arA2b.sbuf_tensor("csq", [64, QT], F32).ap()
        csq2 = arA2b.sbuf_tensor("csq2", [64, QT], F32).ap()
        ropeTq = arA2b.sbuf_tensor("ropeTq", [64, 2 * QT], BF16).ap()
        arRcq = nc.auto_arena()
        cQ = arRcq.sbuf_tensor("cQ", [128, 4, QT], BF16).ap()

        arB = nc.auto_arena()
        maskb = arB.sbuf_tensor("maskb", [128, NKT * 128], U8).ap()
        ztile = arB.sbuf_tensor("ztile", [128, 128], BF16).ap()
        wukb = [arB.sbuf_tensor(f"wukb{i}", [128, 4, DH], BF16).ap() for i in range(3)]
        wuvb = [arB.sbuf_tensor(f"wuvb{i}", [128, 4, DH], BF16).ap() for i in range(3)]
        probs = [arB.sbuf_tensor(f"probs{i}", [128, QT], BF16).ap() for i in range(3)]
        rn = arB.sbuf_tensor("rn", [1, QT], F32).ap()
        rnb = arB.sbuf_tensor("rnb", [128, 256], F32).ap()
        gidxb = [arB.sbuf_tensor(f"gidxb{i}", [128, 256], BF16).ap() for i in range(2)]
        gacc = [arB.sbuf_tensor(f"gacc{i}", [128, 256], BF16).ap() for i in range(2)]
        gtmp = [arB.sbuf_tensor(f"gtmp{i}", [128, 256], BF16).ap()
                for i in range(2)]

        arC = nc.auto_arena()
        z = arC.sbuf_tensor("z", [128, 16, QT], F32).ap()
        hbufC = arC.sbuf_tensor("hbufC", [128, H], F32).ap()
        arC1 = nc.auto_arena()
        wotile = [arC1.sbuf_tensor(f"wot{i}", [128, 16, 128], BF16).ap() for i in range(2)]
        arC2 = nc.auto_arena()
        zln = arC2.sbuf_tensor("zln", [128, 16, QT], BF16).ap()
        actb = arC2.sbuf_tensor("actb", [128, 43, 512], BF16).ap()
        sqz = arC2.sbuf_tensor("sqz", [128, QT], F32).ap()
        gsl = arC2.sbuf_tensor("gsl", [128, 512], BF16).ap()
        l2s = arC2.sbuf_tensor("l2s", [1, 2 * QT], F32).ap()
        isdt = arC2.sbuf_tensor("isdt", [1, QT], F32).ap()
        wgt = [arC2.sbuf_tensor(f"wgt{i}", [128, 16, 128], BF16).ap() for i in range(2)]
        wut = [arC2.sbuf_tensor(f"wut{i}", [128, 16, 128], BF16).ap() for i in range(2)]
        wdt = [arC2.sbuf_tensor(f"wdt{i}", [128, 43, 128], BF16).ap() for i in range(2)]
        arC3 = nc.auto_arena()
        outb = [arC3.sbuf_tensor(f"outb{i}", [128, H], F32).ap() for i in range(2)]

        arL1.allocate("left")
        arRcq.allocate("right")
        arA.allocate("left")
        arA1.allocate("left"); arA1.free()
        arA.free()
        arL2.allocate("left")
        arA2b.allocate("left"); arA2b.free()
        arRcq.free()
        arO.allocate("right")
        arB.allocate("right"); arB.free()
        arL2.free()
        arL1.free()
        arC.allocate("left")
        arC1.allocate("right"); arC1.free()
        arO.free()
        arC2.allocate("left"); arC2.free()
        arC3.allocate("left"); arC3.free()
        arC.free()

        stP1 = ExitStack()
        psA = [stP1.enter_context(nc.psum_tensor(f"psA{i}", [128, 256], F32)).ap()
               for i in range(4)]
        psB = stP1.enter_context(nc.psum_tensor("psB", [64, 256], F32)).ap()
        psB2 = stP1.enter_context(nc.psum_tensor("psB2", [64, 256], F32)).ap()
        psLut = stP1.enter_context(nc.psum_tensor("psLut", [128, 16], F32)).ap()
        stP1.close()
        stP2 = ExitStack()
        psC = stP2.enter_context(nc.psum_tensor("psC", [128, 1024], F32)).ap()
        psR = stP2.enter_context(nc.psum_tensor("psR", [64, 1024], F32)).ap()
        psR2 = stP2.enter_context(nc.psum_tensor("psR2", [64, 1024], F32)).ap()
        stP2.close()
        stP3 = ExitStack()
        psSa = stP3.enter_context(nc.psum_tensor("psSa", [128, 1024], F32)).ap()
        psSb = stP3.enter_context(nc.psum_tensor("psSb", [128, 1024], F32)).ap()
        psO = stP3.enter_context(nc.psum_tensor("psO", [128, 1024], F32)).ap()
        psSum = stP3.enter_context(nc.psum_tensor("psSum", [1, 1024], F32)).ap()
        stP3.close()
        stP4 = ExitStack()
        psW = [stP4.enter_context(nc.psum_tensor(f"psW{i}", [128, 1024], F32)).ap()
               for i in range(2)]
        ps3 = stP4.enter_context(nc.psum_tensor("ps3", [128, 512], F32)).ap()
        psT = [stP4.enter_context(nc.psum_tensor(f"psT{i}", [128, 128], F32)).ap()
               for i in range(2)]
        stP4.close()

        def walk():
            p.dma("sync", identf, i_identf.ap())
            p.dma("sync", onesrf, i_onesrow_f.ap())
            p.dma("sync", cosks, d_coskv)
            p.dma("sync", cosks2, d_coskv2)
            p.dma("sync", mrg1, i_m1.ap())
            ld_pre = p.dma("sync", mrg2, i_m2.ap())
            p.dma("gpsimd", onesb, i_ones_bf.ap(), sem="gld")
            p.dma("gpsimd", onesf, i_ones_f.ap(), sem="gld")
            p.dma("gpsimd", lutsb, d_lut, sem="gld")
            p.dma("gpsimd", wdkvb, d_wdkv.rearrange("(dc q) l -> q dc l", q=128), sem="gld")
            p.dma("gpsimd", wkrb, d_wkr.rearrange("(dc q) r -> q dc r", q=128), sem="gld")
            gld_w = p.dma("gpsimd", wdqb, d_wdq.rearrange("(dc q) l -> q dc l", q=128), sem="gld")

            p.wait("vector", "gld", gld_w)
            p.vec("memset", ebias[:], EXP_BIAS, sem="sLut")
            p.wait("tensor", "gld", gld_w)
            p.wait("tensor", "ld", ld_pre)
            p.mm(psLut[:], onesrf[:], lutsb[:], True, True, sem="sLut")
            p.wait("vector", "sLut", p.cnt["sLut"])
            p.vec("tensor_copy", lutb[:], psLut[:], sem="sLut")
            lut_ready = p.cnt["sLut"]

            last_hbuf = [0, 0]
            last_hbuf_sq = [0, 0]
            last_hln = [0, 0]
            last_hlnT_pa = 0

            def ln_tile(dram_rows, par):
                c0 = par * 6
                p.wait("sync", "sLN", last_hbuf[par])
                p.wait("sync", "sSq", last_hbuf_sq[par])
                lv = p.dma("sync", hbuf[par][:], dram_rows)
                p.wait("vector", "ld", lv)
                p.vec("reduce_sum", stat[:, c0:c0 + 1], hbuf[par][:], axis=AXX)
                muv = p.mark("vector", "sLN")
                p.wait("scalar", "ld", lv)
                p.wait("scalar", "tp", last_hln[par])
                p.act(hln[par][:], hbuf[par][:], AF.Square,
                      accum_out=stat[:, c0 + 1:c0 + 2], sem="sSq")
                sqv = p.cnt["sSq"]
                last_hbuf_sq[par] = sqv
                p.wait("vector", "sLN", muv)
                p.wait("vector", "sSq", sqv)
                p.vec("tensor_scalar", stat[:, c0:c0 + 1], stat[:, c0:c0 + 1],
                      1.0 / H, None, op0=OP.mult)
                mv = p.mark("vector", "sLN")
                p.vec("tensor_scalar", stat[:, c0 + 1:c0 + 2], stat[:, c0 + 1:c0 + 2],
                      1.0 / H, EPS, op0=OP.mult, op1=OP.add)
                s1v = p.mark("vector", "sLN")
                p.wait("scalar", "sLN", mv)
                p.act(stat[:, c0 + 2:c0 + 3], stat[:, c0:c0 + 1], AF.Square, sem="sSq")
                p.wait("vector", "sSq", p.cnt["sSq"])
                p.wait("vector", "sLN", s1v)
                p.vec("tensor_sub", stat[:, c0 + 4:c0 + 5], stat[:, c0 + 1:c0 + 2],
                      stat[:, c0 + 2:c0 + 3])
                vv = p.mark("vector", "sLN")
                p.wait("scalar", "sLN", vv)
                p.act(stat[:, c0 + 4:c0 + 5], stat[:, c0 + 4:c0 + 5], AF.Sqrt, sem="sSq")
                p.wait("vector", "sSq", p.cnt["sSq"])
                p.vec("reciprocal", stat[:, c0 + 3:c0 + 4], stat[:, c0 + 4:c0 + 5])
                rv = p.mark("vector", "sLN")
                p.wait("vector", "sLN", rv)
                p.wait("vector", "tp", last_hln[par])
                p.vec("tensor_scalar", hln[par][:], hbuf[par][:],
                      stat[:, c0:c0 + 1], stat[:, c0 + 3:c0 + 4],
                      op0=OP.subtract, op1=OP.mult)
                lnv = p.mark("vector", "sLN")
                last_hbuf[par] = lnv
                return lnv

            def tr_tile(par, sub, lnv):
                p.wait("scalar", "sLN", lnv)
                p.wait("scalar", "sPA", last_hlnT_pa)
                inst = None
                if p.on("scalar"):
                    inst = p.eng.dma_start(
                        out=hlnT[:, :, sub * 128:(sub + 1) * 128],
                        in_=hln[par].rearrange("t (c d) -> t c d", c=16),
                        transpose=True)
                tv = p.inc("tp", 16, inst)
                last_hln[par] = tv
                return tv

            for g in range(4):
                tpv = 0
                for s_ in range(2):
                    tile = g * 2 + s_
                    lnv = ln_tile(d_hkv[tile * 128:(tile + 1) * 128, :], s_)
                    tpv = tr_tile(s_, s_, lnv)
                p.wait("tensor", "tp", tpv)
                p.wait("tensor", "sEvA", p.cnt["sEvA"])
                for lc in range(4):
                    for dc in range(16):
                        p.mm(psA[lc][:, 0:256],
                             wdkvb[:, dc, lc * 128:(lc + 1) * 128],
                             hlnT[:, dc, 0:256], dc == 0, dc == 15)
                for dc in range(16):
                    p.mm(psB[:, 0:256], wkrb[:, dc, :], hlnT[:, dc, 0:256],
                         dc == 0, dc == 15, sem="sPA" if dc == 15 else None)
                last_hlnT_pa = p.cnt["sPA"]
                pa_g = p.cnt["sPA"]
                p.wait("vector", "sPA", pa_g)
                t0 = g * 256
                for lc in range(4):
                    p.vec("tensor_copy", cKV[:, lc, t0:t0 + 256], psA[lc][:, 0:256])
                p.vec("tensor_mul", ropeT[0:32, 0:256], psB[0:32, 0:256],
                      cosks[0:32, t0:t0 + 256])
                p.vec("tensor_mul", ropeT[32:64, 0:256], psB[32:64, 0:256],
                      cosks[32:64, t0:t0 + 256])
                p.vec("tensor_mul", ropeT[0:32, 256:512], psB[0:32, 0:256],
                      cosks2[0:32, t0:t0 + 256])
                p.vec("tensor_mul", ropeT[32:64, 256:512], psB[32:64, 0:256],
                      cosks2[32:64, t0:t0 + 256], sem="sEvA")
                p.wait("tensor", "sEvA", p.cnt["sEvA"])
                p.mm(psB2[:, 0:256], mrg1[:], ropeT[:, 0:256], True, False)
                p.mm(psB2[:, 0:256], mrg2[:], ropeT[:, 256:512], False, True,
                     sem="sPA")
                p.wait("vector", "sPA", p.cnt["sPA"])
                p.vec("tensor_copy", kRb[:, t0:t0 + 256], psB2[:, 0:256], sem="sEvA")

                tpv = 0
                for s_ in range(2):
                    tile = g * 2 + s_
                    lnv = ln_tile(d_hq[tile * 128:(tile + 1) * 128, :], s_)
                    tpv = tr_tile(s_, s_, lnv)
                p.wait("tensor", "tp", tpv)
                p.wait("tensor", "sEvA", p.cnt["sEvA"])
                for lc in range(4):
                    for dc in range(16):
                        p.mm(psA[lc][:, 0:256],
                             wdqb[:, dc, lc * 128:(lc + 1) * 128],
                             hlnT[:, dc, 0:256], dc == 0, dc == 15,
                             sem="sPA" if (dc == 15 and lc == 3) else None)
                last_hlnT_pa = p.cnt["sPA"]
                p.wait("vector", "sPA", p.cnt["sPA"])
                for lc in range(4):
                    p.vec("tensor_copy", cQ[:, lc, t0:t0 + 256], psA[lc][:, 0:256],
                          sem="sEvA" if lc == 3 else None)

            keys_pa = p.cnt["sPA"]
            keys_ev = p.cnt["sEvA"]
            p.wait("gpsimd", "sPA", keys_pa)
            p.wait("gpsimd", "sEvA", keys_ev)
            kv_st = p.dma("gpsimd", cc_kvin.ap(), kRb[:, 0:QT], sem="gld")
            p.wait("gpsimd", "gld", kv_st)
            inst = None
            if p.on("gpsimd"):
                inst = p.eng.collective_compute(
                    "AllGather", OP.bypass,
                    replica_groups=[list(range(NCORE))],
                    ins=[cc_kvin.ap().opt()],
                    outs=[cc_kvout.ap().opt()])
            p.inc("sKV", 1, inst)

            hq_pa = p.cnt["sPA"]
            hq_ev = p.cnt["sEvA"]
            p.wait("gpsimd", "sPA", hq_pa)
            p.wait("gpsimd", "sEvA", hq_ev)
            p.dma("gpsimd", wuqb, d_wuq.rearrange("(lc q) d -> q lc d", q=128), sem="gld")
            gld_q2 = p.dma("gpsimd", wqrb, d_wqr.rearrange("(lc q) d -> q lc d", q=128), sem="gld")
            p.dma("sync", csq, d_cosq)
            csq_v = p.dma("sync", csq2, d_cosq2)
            p.wait("tensor", "gld", gld_q2)
            p.wait("vector", "ld", csq_v)

            p.wait("tensor", "sEvA", p.cnt["sEvA"])
            for h_ in range(NH):
                if h_ > 0:
                    p.wait("tensor", "sEvA", p.cnt["sEvA"])
                for half in range(2):
                    for lc in range(4):
                        p.mm(psC[:, half * 512:(half + 1) * 512],
                             wuqb[:, lc, h_ * 128:(h_ + 1) * 128],
                             cQ[:, lc, half * 512:(half + 1) * 512],
                             lc == 0, lc == 3)
                    for lc in range(4):
                        p.mm(psR[:, half * 512:(half + 1) * 512],
                             wqrb[:, lc, h_ * RD:(h_ + 1) * RD],
                             cQ[:, lc, half * 512:(half + 1) * 512],
                             lc == 0, lc == 3,
                             sem="sPA" if (lc == 3 and half == 1) else None)
                p.wait("vector", "sPA", p.cnt["sPA"])
                p.vec("tensor_scalar", QC[:, h_ * QT:(h_ + 1) * QT], psC[:],
                      SCALE, None, op0=OP.mult)
                p.vec("tensor_mul", ropeTq[0:32, 0:QT], psR[0:32, :], csq[0:32, :])
                p.vec("tensor_mul", ropeTq[32:64, 0:QT], psR[32:64, :], csq[32:64, :])
                p.vec("tensor_mul", ropeTq[0:32, QT:2 * QT], psR[0:32, :], csq2[0:32, :])
                p.vec("tensor_mul", ropeTq[32:64, QT:2 * QT], psR[32:64, :],
                      csq2[32:64, :], sem="sEvA")
                p.wait("tensor", "sEvA", p.cnt["sEvA"])
                for half in range(2):
                    p.mm(psR2[:, half * 512:(half + 1) * 512], mrg1[:],
                         ropeTq[:, half * 512:(half + 1) * 512], True, False)
                    p.mm(psR2[:, half * 512:(half + 1) * 512], mrg2[:],
                         ropeTq[:, QT + half * 512:QT + (half + 1) * 512], False, True,
                         sem="sPA" if half == 1 else None)
                p.wait("vector", "sPA", p.cnt["sPA"])
                p.vec("tensor_scalar", QR[:, h_ * QT:(h_ + 1) * QT], psR2[:],
                      SCALE, None, op0=OP.mult, sem="sEvA")

            p.wait("sync", "sKV", 1)
            kv_rb = p.dma("sync",
                          kRb[:, :].rearrange("q (c t) -> q c t", c=8),
                          cc_kvout.ap().rearrange("c q t -> q c t"))
            p.wait("sync", "ld", kv_rb)

            p.barrier()
            if stop_after == "A":
                return

            if debug:
                p.dma("gpsimd", taps["t_ckv"], cKV.rearrange("q a b -> q (a b)"), sem="gld")
                p.dma("gpsimd", taps["t_kr"], kRb[:, :], sem="gld")
                p.dma("gpsimd", taps["t_qc"], QC[:, :], sem="gld")
                p.dma("gpsimd", taps["t_qr"], QR[:, :], sem="gld")

            ml = p.dma("sync", maskb, d_masks)
            p.wait("vector", "ld", ml)
            p.vec("memset", ztile[:], 0.0, sem="sM")
            mask_ready = p.cnt["sM"]

            def load_head_w(h_, gate=0):
                par = h_ % 3
                p.wait("gpsimd", "sExp", gate)
                p.dma("gpsimd", wukb[par],
                      d_wuk.rearrange("(lc q) d -> q lc d", q=128)[:, :, h_ * DH:(h_ + 1) * DH],
                      sem="gld")
                return p.dma("gpsimd", wuvb[par],
                             d_wuv.rearrange("(lc q) d -> q lc d", q=128)[:, :, h_ * DH:(h_ + 1) * DH],
                             sem="gld")

            wk_ready = [0] * NH
            wk_ready[0] = load_head_w(0)
            wk_ready[1] = load_head_w(1)
            N_UNITS = 3 * 43
            g_state = {
                "loaded": 0,
                "idx_ready": [0] * N_UNITS,
                "store_v": [0, 0],
                "vec_done": 0,
            }

            def gather_prefetch():
                while (g_state["loaded"] < N_UNITS
                       and g_state["loaded"] < g_state["vec_consumed_units"] + 2):
                    k = g_state["loaded"]
                    mi, u = divmod(k, 43)
                    dsh = (d_gsh, d_ush, d_dsh)[mi]
                    slot = k % 2
                    p.wait("gpsimd", "sGat", max(0, (k - 2 + 1)))
                    g_state["idx_ready"][k] = p.dma(
                        "gpsimd", gidxb[slot][:],
                        dsh[:, u * 256:(u + 1) * 256], sem="gld")
                    g_state["loaded"] += 1

            def gather_steps():
                for k in range(N_UNITS):
                    mi, u = divmod(k, 43)
                    slot = k % 2
                    aslot = k % 2
                    def first(k=k, slot=slot, aslot=aslot):
                        p.wait("vector", "gld", g_state["idx_ready"][k])
                        p.wait("vector", "sLut", lut_ready)
                        p.wait("vector", "gld", g_state["store_v"][aslot])
                        p.vec("tensor_scalar", gacc[aslot][:], gidxb[slot][:],
                              0.0, lutb[:, 0:1], op0=OP.is_equal, op1=OP.mult)
                    yield first

                    def eqk(kk, slot=slot):
                        p.vec("tensor_scalar", gtmp[kk % 2][:], gidxb[slot][:],
                              float(kk), lutb[:, kk:kk + 1],
                              op0=OP.is_equal, op1=OP.mult)

                    def addk(kk, aslot=aslot):
                        p.vec("tensor_add", gacc[aslot][:],
                              gacc[aslot][:], gtmp[kk % 2][:])

                    yield (lambda: eqk(1))
                    for kk in range(2, 16):
                        yield (lambda kk=kk: eqk(kk))
                        yield (lambda kk=kk: addk(kk - 1))
                    yield (lambda: addk(15))
                    def fin(k=k, mi=mi, u=u, aslot=aslot):
                        gv = p.mark("vector", "sGat")
                        g_state["vec_consumed_units"] = k + 1
                        p.wait("gpsimd", "sGat", gv)
                        g_state["store_v"][aslot] = p.dma(
                            "gpsimd",
                            cc_in[mi].ap()[u * 2:u * 2 + 2, :, :].rearrange(
                                "tl q o -> q tl o"),
                            gacc[aslot][:], sem="gld")
                        gather_prefetch()
                    yield fin

            g_state["vec_consumed_units"] = 0
            g_iter = gather_steps()
            g_done = [False]
            GATHER_HEADS = GATHER_HEADS_CFG
            G_STEPS_TOTAL = N_UNITS * 32 + 1
            g_per_tile = -(-G_STEPS_TOTAL // (GATHER_HEADS * 64))

            def emit_gather_quota(nsteps):
                for _ in range(nsteps):
                    stp = next(g_iter, None)
                    if stp is None:
                        g_done[0] = True
                        return
                    stp()

            last_exp = [0, 0]
            n_iter = 0
            head_evac = 0

            gather_prefetch()
            ag_issued = [False]

            exp_done = [0] * NH

            def emit_expansion(hx):
                parw = hx % 3
                p.wait("tensor", "gld", wk_ready[hx])
                p.wait("tensor", "sP", max(last_exp))
                p.wait("tensor", "sHd", head_evac)
                p.wait("tensor", "sEvc", p.cnt["sEvc"])
                for half in range(2):
                    for lc in range(4):
                        p.mm(psSb[:, half * 512:(half + 1) * 512],
                             wukb[parw][:, lc, :],
                             cKV[:, lc, half * 512:(half + 1) * 512],
                             lc == 0, lc == 3)
                for tl in range(8):
                    for lc in range(4):
                        p.mm(psSa[:, tl * 128:(tl + 1) * 128],
                             cKV[:, lc, tl * 128:(tl + 1) * 128],
                             wuvb[parw][:, lc, :], lc == 0, lc == 3,
                             sem="sExp" if (tl == 7 and lc == 3) else None)
                exp_done[hx] = p.cnt["sExp"]
                p.wait("vector", "sExp", exp_done[hx])
                p.vec("tensor_copy", kstage[:, 0:1024], psSb[:])
                p.vec("tensor_copy", kstage[:, 1024:2048], psSa[:], sem="sEvc")
                evc = p.cnt["sEvc"]
                p.wait("gpsimd", "sEvc", evc)
                if hx >= 2:
                    p.wait("gpsimd", "sKX", hx - 1)
                p.dma("gpsimd", cc_xin[hx % 2].ap()[0], kstage[:, 0:1024],
                      sem="gld")
                stv = p.dma("gpsimd", cc_xin[hx % 2].ap()[1],
                            kstage[:, 1024:2048], sem="gld")
                p.wait("gpsimd", "gld", stv)
                if hx >= 2:
                    p.wait("gpsimd", "sF", (hx - 1) * 64)
                inst = None
                if p.on("gpsimd"):
                    inst = p.eng.collective_compute(
                        "AllGather", OP.bypass,
                        replica_groups=[list(range(NCORE))],
                        ins=[cc_xin[hx % 2].ap().opt()],
                        outs=[cc_xout[hx % 2].ap().opt()])
                p.inc("sKX", 1, inst)
                return evc

            kt_free = [0, 0]
            kt_ready = {}

            def load_kt(k):
                h, tc8 = divmod(k, 8)
                slot = k % 2
                p.wait("sync", "sKX", h + 1)
                p.wait("sync", "sF", kt_free[slot])
                p.dma("sync", KCb[slot][:], cc_xout[h % 2].ap()[2 * tc8])
                kt_ready[(h, tc8)] = p.dma("sync", Vb[slot][:],
                                           cc_xout[h % 2].ap()[2 * tc8 + 1])
                kt_free[slot] = h * 64 + (tc8 + 1) * 8

            def maybe_issue_ag(h=None):
                if ag_issued[0] or not g_done[0]:
                    return
                if h is not None and h < AG_MIN_HEAD:
                    return
                ag_issued[0] = True
                p.wait("gpsimd", "gld", p.cnt["gld"])
                for mi in range(3):
                    inst = None
                    if p.on("gpsimd"):
                        inst = p.eng.collective_compute(
                            "AllGather", OP.bypass,
                            replica_groups=[list(range(NCORE))],
                            ins=[cc_in[mi].ap().opt()],
                            outs=[cc_out[mi].ap().opt()])
                    p.inc("sAG", 1, inst)

            NHL = nh_lim or NH
            emit_expansion(0)
            load_kt(0)
            load_kt(1)

            for h_ in range(NHL):
                if h_ + 2 < NHL:
                    wk_ready[h_ + 2] = load_head_w(
                        h_ + 2, gate=exp_done[h_ - 1] if h_ >= 1 else 0)
                if h_ + 1 < NHL:
                    evc_gate = emit_expansion(h_ + 1)
                else:
                    evc_gate = p.cnt["sEvc"]

                for tc8 in range(8):
                    kt = h_ * 8 + tc8
                    if kt + 2 < NHL * 8:
                        load_kt(kt + 2)
                    kslot = kt % 2
                    p.wait("tensor", "ld", kt_ready[(h_, tc8)])
                    p.wait("tensor", "sEvc", evc_gate)
                    if tc8 == 0 and h_ > 0:
                        p.wait("tensor", "sHd", head_evac)

                    for tp_ in range(4):
                        pair = []
                        for tl in (tp_ * 2, tp_ * 2 + 1):
                            t_ = tc8 * 8 + tl
                            n = n_iter
                            n_iter += 1
                            jmin = t_ // 8
                            c0 = jmin * 128
                            par = n % 2
                            psS = psSa if par == 0 else psSb
                            segs = []
                            lo = c0
                            while lo < QT:
                                hi = min((lo // 512 + 1) * 512, QT)
                                segs.append((lo, hi))
                                lo = hi
                            p.wait("tensor", "sP", last_exp[par])
                            qb = h_ * QT
                            for si, (lo, hi) in enumerate(segs):
                                p.mm(psS[:, lo:hi],
                                     KCb[kslot][:, tl * 128:(tl + 1) * 128],
                                     QC[:, qb + lo:qb + hi], True, False)
                                p.mm(psS[:, lo:hi], kRb[:, t_ * 128:(t_ + 1) * 128],
                                     QR[:, qb + lo:qb + hi], False, True,
                                     sem="sS" if si == len(segs) - 1 else None)
                            sv = p.cnt["sS"]
                            pb = probs[n % 3]
                            p.wait("scalar", "sS", sv)
                            if h_ == 0 and t_ == 0:
                                p.wait("scalar", "sLut", lut_ready)
                            if n >= 3:
                                p.wait("scalar", "sF", n - 2)
                            p.act(pb[:, c0:QT], psS[:, c0:QT], AF.Exp, bias=ebias[:],
                                  sem="sP")
                            last_exp[par] = p.cnt["sP"]
                            p.wait("vector", "sP", p.cnt["sP"])
                            p.wait("vector", "sM", mask_ready)
                            p.vec("copy_predicated", pb[:, c0:c0 + 128],
                                  maskb[:, t_ * 128:(t_ + 1) * 128], ztile[:], sem="sM")
                            pair.append((t_, tl, segs, pb, p.cnt["sM"]))
                        if h_ < GATHER_HEADS and not g_done[0]:
                            emit_gather_quota(2 * g_per_tile)
                        for (t_, tl, segs, pb, mv) in pair:
                            p.wait("tensor", "sM", mv)
                            first = (t_ == 0)
                            last = (t_ == NKT - 1)
                            for (lo, hi) in segs:
                                p.mm(psSum[:, lo:hi], onesb[:], pb[:, lo:hi], first, last)
                            for (lo, hi) in segs:
                                p.mm(psO[:, lo:hi],
                                     Vb[kslot][:, tl * 128:(tl + 1) * 128],
                                     pb[:, lo:hi], first, last)
                            p.inc("sF", 1, p.eng.nop() if p.on("tensor") else None)

                maybe_issue_ag(h_)
                p.wait("vector", "sF", p.cnt["sF"])
                p.vec("reciprocal", rn[:], psSum[:])
                rv = p.mark("vector", "sHd")
                p.wait("tensor", "sHd", rv)
                p.wait("tensor", "sP", max(last_exp))
                for half in range(2):
                    p.mm(psSa[:, half * 512:(half + 1) * 512], onesrf[:],
                         rn[:, half * 512:(half + 1) * 512], True, True,
                         sem="sHd" if half == 1 else None)
                bv = p.cnt["sHd"]
                p.wait("vector", "sHd", bv)
                for qq in range(4):
                    p.vec("tensor_copy", rnb[:], psSa[:, qq * 256:(qq + 1) * 256])
                    p.vec("tensor_mul",
                          o_all[:, h_ * QT + qq * 256:h_ * QT + (qq + 1) * 256],
                          psO[:, qq * 256:(qq + 1) * 256], rnb[:],
                          sem="sHd" if qq == 3 else None)
                head_evac = p.cnt["sHd"]
                last_exp = [p.cnt["sP"], p.cnt["sP"]]

            p.barrier()
            if stop_after == "B":
                return

            if debug:
                p.dma("gpsimd", taps["t_oall"], o_all[:, :], sem="gld")

            psT_last = [0, 0]
            hbufC_free = 0
            for qt in range(8):
                p.wait("sync", "sS", hbufC_free)
                lv = p.dma("sync", hbufC[:], d_hq[qt * 128:(qt + 1) * 128, :])
                p.wait("tensor", "ld", lv)
                for dc in range(16):
                    i = dc % 2
                    p.wait("tensor", "sZ", psT_last[i])
                    p.tr(psT[i][:], hbufC[:, dc * 128:(dc + 1) * 128], identf[:],
                         sem="sS")
                    p.wait("vector", "sS", p.cnt["sS"])
                    zv = p.vec("tensor_copy", z[:, dc, qt * 128:(qt + 1) * 128],
                               psT[i][:], sem="sZ")
                    psT_last[i] = p.cnt["sZ"]
                hbufC_free = p.cnt["sS"]
            zinit = p.cnt["sZ"]
            if debug:
                p.wait("gpsimd", "sZ", zinit)
                p.dma("gpsimd", taps["t_z1"], z.rearrange("q a b -> q (a b)"), sem="gld")

            wo_view = d_wo.rearrange("(dcI q) n -> q dcI n", q=128)
            wo_ready = [0] * 16
            wo_free = [0, 0]
            wo_ready[0] = p.dma("gpsimd", wotile[0], wo_view[:, :, 0:128], sem="gld")
            for dcO in range(16):
                par = dcO % 2
                if dcO + 1 < 16:
                    p.wait("gpsimd", "sS", wo_free[(dcO + 1) % 2])
                    wo_ready[dcO + 1] = p.dma(
                        "gpsimd", wotile[(dcO + 1) % 2],
                        wo_view[:, :, (dcO + 1) * 128:(dcO + 2) * 128], sem="gld")
                p.wait("tensor", "gld", wo_ready[dcO])
                if dcO >= 2:
                    p.wait("tensor", "sZ", zinit + dcO - 1)
                for half in range(2):
                    for dcI in range(16):
                        p.mm(psW[par][:, half * 512:(half + 1) * 512],
                             wotile[par][:, dcI, :],
                             o_all[:, dcI * QT + half * 512:dcI * QT + (half + 1) * 512],
                             dcI == 0, dcI == 15,
                             sem="sS" if (dcI == 15 and half == 1) else None)
                wo_free[par] = p.cnt["sS"]
                p.wait("vector", "sS", p.cnt["sS"])
                p.wait("vector", "sZ", zinit)
                p.vec("tensor_add", z[:, dcO, :], z[:, dcO, :], psW[par][:], sem="sZ")
            zdone = p.cnt["sZ"]
            if debug:
                p.wait("gpsimd", "sZ", zdone)
                p.dma("gpsimd", taps["t_z2"], z.rearrange("q a b -> q (a b)"), sem="gld")

            sq_free = 0
            for dc in range(16):
                p.wait("scalar", "sZ", zinit + 1 + dc)
                p.wait("scalar", "sL2", sq_free)
                p.act(sqz[:], z[:, dc, :], AF.Square, sem="sSq")
                p.wait("tensor", "sSq", p.cnt["sSq"])
                p.wait("tensor", "sZ", zinit + 1 + dc)
                for half in range(2):
                    p.mm(psW[0][0:1, half * 512:(half + 1) * 512], onesf[:],
                         z[:, dc, half * 512:(half + 1) * 512], dc == 0, dc == 15)
                    p.mm(psW[1][0:1, half * 512:(half + 1) * 512], onesf[:],
                         sqz[:, half * 512:(half + 1) * 512], dc == 0, dc == 15,
                         sem="sL2" if half == 1 else None)
                sq_free = p.cnt["sL2"]
            p.wait("vector", "sL2", p.cnt["sL2"])
            mu_ = l2s[:, 0:QT]
            var_ = l2s[:, QT:2 * QT]
            tmp_ = sqz[0:1, 0:QT]
            isd_ = isdt[0:1, :]
            p.vec("tensor_scalar", mu_, psW[0][0:1, :], 1.0 / H, None, op0=OP.mult)
            mv = p.mark("vector", "sL2")
            p.vec("tensor_scalar", var_, psW[1][0:1, :], 1.0 / H, EPS,
                  op0=OP.mult, op1=OP.add)
            s1v = p.mark("vector", "sL2")
            p.wait("scalar", "sL2", mv)
            p.act(tmp_, mu_, AF.Square, sem="sSq")
            p.wait("vector", "sSq", p.cnt["sSq"])
            p.wait("vector", "sL2", s1v)
            p.vec("tensor_sub", var_, var_, tmp_)
            vv = p.mark("vector", "sL2")
            p.wait("scalar", "sL2", vv)
            p.act(var_, var_, AF.Sqrt, sem="sSq")
            p.wait("vector", "sSq", p.cnt["sSq"])
            p.vec("reciprocal", isd_, var_)
            rv = p.mark("vector", "sL2")
            p.wait("tensor", "sL2", rv)
            for half in range(2):
                p.mm(psW[0][:, half * 512:(half + 1) * 512], onesrf[:],
                     mu_[:, half * 512:(half + 1) * 512], True, True)
                p.mm(psW[1][:, half * 512:(half + 1) * 512], onesrf[:],
                     isd_[:, half * 512:(half + 1) * 512], True, True,
                     sem="sL2" if half == 1 else None)
            p.wait("vector", "sL2", p.cnt["sL2"])
            for dc in range(16):
                p.vec("tensor_sub", sqz[:], z[:, dc, :], psW[0][:])
                p.vec("tensor_mul", zln[:, dc, :], sqz[:], psW[1][:],
                      sem="sL2" if dc == 15 else None)
            zln_done = p.cnt["sL2"]
            if debug:
                p.wait("gpsimd", "sL2", zln_done)
                p.dma("gpsimd", taps["t_zln"], zln.rearrange("q a b -> q (a b)"), sem="gld")

            wo_all_done = p.cnt["sS"]
            p.wait("sync", "sS", wo_all_done)
            p.wait("sync", "sAG", 3)
            ccg = cc_out[0].ap().rearrange("(hc oc) q o -> hc oc q o", hc=16)
            if debug:
                p.wait("gpsimd", "sS", wo_all_done)
                p.wait("gpsimd", "sAG", 3)
                p.dma("gpsimd", taps["t_ccge"],
                      cc_out[0].ap().rearrange("(hc oc) q o -> hc oc q o", hc=16)[:, 2],
                      sem="gld")
            ccu = cc_out[1].ap().rearrange("(hc oc) q o -> hc oc q o", hc=16)
            ccd_ = cc_out[2].ap()
            gu_free = [0, 0]
            mult_done = {}
            silu_free = 0
            for qh in range(2):
                q0 = qh * 512
                gu_ready = [0] * 43
                p.wait("sync", "sFF", gu_free[0])
                p.dma("sync", wut[0], ccu[:, 0].rearrange("hc q o -> q hc o"))
                gu_ready[0] = p.dma("sync", wgt[0], ccg[:, 0].rearrange("hc q o -> q hc o"))
                for oc in range(43):
                    par = oc % 2
                    if oc + 1 < 43:
                        p.wait("sync", "sFF", gu_free[(oc + 1) % 2])
                        p.dma("sync", wut[(oc + 1) % 2],
                              ccu[:, oc + 1].rearrange("hc q o -> q hc o"))
                        gu_ready[oc + 1] = p.dma(
                            "sync", wgt[(oc + 1) % 2],
                            ccg[:, oc + 1].rearrange("hc q o -> q hc o"))
                    if debug and qh == 0 and 2 <= oc <= 5:
                        p.wait("gpsimd", "ld", gu_ready[oc])
                        p.dma("gpsimd",
                              taps["t_wgt"][:, (oc - 2) * 2048:(oc - 1) * 2048],
                              wgt[par].rearrange("q a b -> q (a b)"), sem="gld")
                        tv = p.dma("gpsimd",
                                   taps["t_wut"][:, (oc - 2) * 2048:(oc - 1) * 2048],
                                   wut[par].rearrange("q a b -> q (a b)"), sem="gld")
                        p.wait("tensor", "gld", tv)
                    p.wait("tensor", "ld", gu_ready[oc])
                    p.wait("tensor", "sL2", zln_done)
                    key = (qh, oc - 2)
                    if key in mult_done:
                        p.wait("tensor", "sFF", mult_done[key])
                    elif qh > 0 and oc < 2:
                        p.wait("tensor", "sFF", mult_done[(0, 41 + oc)])
                    p.wait("tensor", "sP", silu_free)
                    for hc in range(16):
                        p.mm(psW[0][:, par * 512:par * 512 + 512],
                             wgt[par][:, hc, :], zln[:, hc, q0:q0 + 512],
                             hc == 0, hc == 15)
                    for hc in range(16):
                        p.mm(psW[1][:, par * 512:par * 512 + 512],
                             wut[par][:, hc, :], zln[:, hc, q0:q0 + 512],
                             hc == 0, hc == 15,
                             sem="sFF" if hc == 15 else None)
                    gu_free[par] = p.cnt["sFF"]
                    gv = p.cnt["sFF"]
                    if debug and qh == 0 and oc == 2:
                        p.wait("scalar", "sFF", gv)
                        p.act(sqz[:, 0:512],
                              psW[0][:, par * 512:par * 512 + 512], AF.Identity)
                        p.act(sqz[:, 512:1024],
                              psW[1][:, par * 512:par * 512 + 512], AF.Identity,
                              sem="sFF")
                        p.wait("gpsimd", "sFF", p.cnt["sFF"])
                        p.dma("gpsimd", taps["t_psg"], sqz[:, 0:512], sem="gld")
                        ps_tap_v = p.dma("gpsimd", taps["t_psu"],
                                         sqz[:, 512:1024], sem="gld")
                    if debug and qh == 0 and oc == 4:
                        p.wait("tensor", "gld", ps_tap_v)
                    p.wait("scalar", "sFF", gv)
                    p.act(gsl[:], psW[0][:, par * 512:par * 512 + 512], AF.Silu,
                          sem="sP")
                    silu_free = p.cnt["sP"]
                    p.wait("vector", "sP", p.cnt["sP"])
                    p.vec("tensor_mul", actb[:, oc, :], gsl[:],
                          psW[1][:, par * 512:par * 512 + 512], sem="sFF")
                    mult_done[(qh, oc)] = p.cnt["sFF"]
                if debug and qh == 0:
                    p.wait("gpsimd", "sFF", mult_done[(0, 42)])
                    tv = p.dma("gpsimd", taps["t_act"], actb.rearrange("q a b -> q (a b)"), sem="gld")
                    p.wait("tensor", "gld", tv)
                wd_ready = [0] * 16
                wd_free = [0, 0]
                wd_ready[0] = p.dma("sync", wdt[0],
                                    ccd_[0:43].rearrange("tl q o -> q tl o"))
                for dcO in range(16):
                    par = dcO % 2
                    if dcO + 1 < 16:
                        p.wait("sync", "sFF", wd_free[(dcO + 1) % 2])
                        wd_ready[dcO + 1] = p.dma(
                            "sync", wdt[(dcO + 1) % 2],
                            ccd_[(dcO + 1) * 43:(dcO + 2) * 43].rearrange("tl q o -> q tl o"))
                    p.wait("tensor", "ld", wd_ready[dcO])
                    p.wait("tensor", "sFF", p.cnt["sFF"])
                    for oc in range(43):
                        p.mm(ps3[:], wdt[par][:, oc, :], actb[:, oc, :],
                             oc == 0, oc == 42, sem="sFF" if oc == 42 else None)
                    wd_free[par] = p.cnt["sFF"]
                    p.wait("vector", "sFF", p.cnt["sFF"])
                    p.vec("tensor_add", z[:, dcO, q0:q0 + 512],
                          z[:, dcO, q0:q0 + 512], ps3[:], sem="sFF")

            ffn_done = p.cnt["sFF"]
            if debug:
                p.wait("gpsimd", "sFF", ffn_done)
                p.dma("gpsimd", taps["t_z3"], z.rearrange("q a b -> q (a b)"), sem="gld")
                p.dma("gpsimd", taps["t_oall2"], o_all[:, :], sem="gld")
                p.dma("gpsimd", taps["t_ccg"], cc_out[0].ap()[0:8], sem="gld")
                p.dma("gpsimd", taps["t_ccd"], cc_out[2].ap()[0:8], sem="gld")
            p.wait("tensor", "sFF", ffn_done)
            psT_last = [p.cnt["sZ"], p.cnt["sZ"]]
            outb_free = [0, 0]
            for qt in range(8):
                par = qt % 2
                for dc in range(16):
                    i = dc % 2
                    p.wait("tensor", "sZ", psT_last[i])
                    p.tr(psT[i][:], z[:, dc, qt * 128:(qt + 1) * 128], identf[:],
                         sem="sS")
                    p.wait("vector", "sS", p.cnt["sS"])
                    p.wait("vector", "sOut", outb_free[par])
                    p.vec("tensor_copy", outb[par][:, dc * 128:(dc + 1) * 128],
                          psT[i][:], sem="sZ")
                    psT_last[i] = p.cnt["sZ"]
                p.wait("gpsimd", "sZ", p.cnt["sZ"])
                sv = p.dma("gpsimd", d_out[qt * 128:(qt + 1) * 128, :], outb[par][:],
                           sem="sOut")
                outb_free[par] = sv
            p.wait("gpsimd", "sOut", p.cnt["sOut"])

        with nc.Block() as block:
            @block.sync
            def _(eng):
                p.E, p.eng = "sync", eng
                p.reset()
                walk()

            @block.gpsimd
            def _(eng):
                p.E, p.eng = "gpsimd", eng
                p.reset()
                walk()

            @block.tensor
            def _(eng):
                p.E, p.eng = "tensor", eng
                p.reset()
                walk()

            @block.vector
            def _(eng):
                p.E, p.eng = "vector", eng
                p.reset()
                walk()

            @block.scalar
            def _(eng):
                p.E, p.eng = "scalar", eng
                p.reset()
                walk()

    return nc


def _host_inputs(inputs):
    h = np.ascontiguousarray(np.asarray(inputs["L0_attn_h"]).reshape(S, H))
    lut = np.ascontiguousarray(np.asarray(inputs["L0_ffn_lut"]).reshape(1, 16))

    def idx8(a):
        a = np.ascontiguousarray(np.asarray(a)).view(np.int8)
        return a.reshape(a.shape[0], -1, 4)[:, :, 0]

    gidx = idx8(inputs["L0_ffn_gate_idx"])
    uidx = idx8(inputs["L0_ffn_up_idx"])
    didx = idx8(inputs["L0_ffn_down_idx"])

    def tiles_gu(a):
        t = np.ascontiguousarray(a.T)
        t = t.reshape(16, 128, 43, 128)
        return np.ascontiguousarray(t.transpose(0, 2, 1, 3).reshape(NTILE, 128, 128))

    def tiles_d(a):
        t = np.ascontiguousarray(a.T)
        t = t.reshape(43, 128, 16, 128)
        return np.ascontiguousarray(t.transpose(2, 0, 1, 3).reshape(NTILE, 128, 128))

    gt = tiles_gu(gidx)
    ut = tiles_gu(uidx)
    dt_ = tiles_d(didx)

    theta = 1.0 / 10000.0 ** (np.arange(0, RD, 2, dtype=np.float64) / RD)
    perm = np.concatenate([np.arange(0, RD, 2), np.arange(1, RD, 2)])
    wkr = np.ascontiguousarray(np.asarray(inputs["L0_attn_W_KR"])[:, perm])
    wqr_ = np.asarray(inputs["L0_attn_W_QR"]).reshape(LL, NH, RD)
    wqr = np.ascontiguousarray(wqr_[:, :, perm].reshape(LL, NH * RD))

    shared = {
        "wdkv": np.ascontiguousarray(np.asarray(inputs["L0_attn_W_DKV"])),
        "wuk": np.ascontiguousarray(np.asarray(inputs["L0_attn_W_UK"])),
        "wkr": wkr,
        "wuv": np.ascontiguousarray(np.asarray(inputs["L0_attn_W_UV"])),
        "wdq": np.ascontiguousarray(np.asarray(inputs["L0_attn_W_DQ"])),
        "wuq": np.ascontiguousarray(np.asarray(inputs["L0_attn_W_UQ"])),
        "wqr": wqr,
        "wo": np.ascontiguousarray(np.asarray(inputs["L0_attn_W_O"])),
        "lut16": lut,
    }

    in_maps, all_pos = [], []
    for c in range(NCORE):
        chunks = [c + 8 * j for j in range(8)]
        posn = np.concatenate([np.arange(g * 128, (g + 1) * 128) for g in chunks])
        all_pos.append(posn)

        def shard(tl):
            ss = tl[SH_T * c:SH_T * (c + 1)]
            return np.ascontiguousarray(ss.transpose(1, 0, 2).reshape(128, SH_T * 128))

        m = np.zeros((NKT, 128, 128), dtype=np.uint8)
        ii = np.arange(128)
        for t in range(NKT):
            jmin = t // 8
            g = c + 8 * jmin
            if t > g:
                m[t, :, :] = 1
            elif t == g:
                m[t] = (ii[:, None] > ii[None, :]).astype(np.uint8)
        masks = np.ascontiguousarray(m.transpose(1, 0, 2).reshape(128, NKT * 128))
        ang = posn[None, :].astype(np.float64) * theta[:, None]
        csq = np.concatenate([np.cos(ang), np.sin(ang)], axis=0).astype(np.float32)
        csq2 = np.concatenate([np.sin(ang), np.cos(ang)], axis=0).astype(np.float32)
        posk = np.arange(QT * c, QT * (c + 1), dtype=np.float64)
        angk = posk[None, :] * theta[:, None]
        csk = np.concatenate([np.cos(angk), np.sin(angk)], axis=0)
        csk2 = np.concatenate([np.sin(angk), np.cos(angk)], axis=0)
        mp = dict(shared)
        mp.update({
            "hq": np.ascontiguousarray(h[posn]),
            "hkv": np.ascontiguousarray(h[QT * c:QT * (c + 1)]),
            "gsh": shard(gt), "ush": shard(ut), "dsh": shard(dt_),
            "masks": masks,
            "cosq": np.ascontiguousarray(csq),
            "cosq2": np.ascontiguousarray(csq2),
            "coskv": csk.astype(ml_dtypes.bfloat16),
            "coskv2": csk2.astype(ml_dtypes.bfloat16),
        })
        in_maps.append(mp)
    return in_maps, all_pos


_PROGRAM_CACHE = {}


def kernel(**inputs):
    in_maps, all_pos = _host_inputs(inputs)
    if "nc" not in _PROGRAM_CACHE:
        _PROGRAM_CACHE["nc"] = build_program()
    nc = _PROGRAM_CACHE["nc"]
    res = run_bass_kernel_spmd(nc, in_maps, core_ids=list(range(NCORE)))
    out = np.empty((S, H), dtype=np.float32)
    for c in range(NCORE):
        out[all_pos[c]] = res.results[c]["out"]
    return out.reshape(1, S, H)
